# version 07
# baseline (speedup 1.0000x reference)
import numpy as np
import ml_dtypes
import concourse.bass as bass
import concourse.mybir as mybir
from concourse.bass_utils import run_bass_kernel_spmd
from concourse.library_config import mlp
from concourse.library_overlay import lower_extended_insts
from contextlib import ExitStack

F32 = mybir.dt.float32
F32R = mybir.dt.float32r
BF16 = mybir.dt.bfloat16
F16 = mybir.dt.float16
I32 = mybir.dt.int32
I16 = mybir.dt.int16
AF = mybir.ActivationFunctionType
ALU = mybir.AluOpType
AX = mybir.AxisListType

PI = float(np.pi)
TWO_PI = float(2 * np.pi)

D = 1024
TOTAL = 2080
KH = 32
MT = 4
T = 4096
TO = 2048
HH = 2048
E = 8
CAP = 512
HF = 5464
NSLOT = E * CAP + 1

ENGS = ("sync", "gpsimd", "tensor", "vector", "scalar")
NDPOOL = 6

DBG = []
STOP_PHASE = 99


class Sched:

    GAP = 12

    def __init__(self):
        self.ops = []
        self.ccount = {e: 0 for e in ENGS}
        self.dcount = {}
        self.dpool_i = {e: 0 for e in ENGS}
        self.ordc = {e: 0 for e in ENGS}
        self.last_w = {}
        self.readers = {}
        self.targets = set()

    def _collect(self, eng, reads, writes):
        waits = {}
        cur = self.ordc[eng]
        def need(tok, peng, pdma, pord):
            if peng == eng and not pdma:
                if peng == "tensor" or (cur - pord) >= self.GAP:
                    return
            sem, val = tok
            if waits.get(sem, 0) < val:
                waits[sem] = val
            self.targets.add(tok)
        for k in reads:
            if k in self.last_w:
                tok, we, wd, wo = self.last_w[k]
                need(tok, we, wd, wo)
        for k in writes:
            if k in self.last_w:
                tok, we, wd, wo = self.last_w[k]
                need(tok, we, wd, wo)
            for re_, (tok, rd, ro) in self.readers.get(k, {}).items():
                need(tok, re_, rd, ro)
        return waits

    def op(self, eng, fn, reads=(), writes=()):
        waits = self._collect(eng, reads, writes)
        self.ccount[eng] += 1
        self.ordc[eng] += 1
        tok = ("c_" + eng, self.ccount[eng])
        o = self.ordc[eng]
        self.ops.append((eng, fn, waits, tok, False))
        for k in reads:
            self.readers.setdefault(k, {})[eng] = (tok, False, o)
        for k in writes:
            self.last_w[k] = (tok, eng, False, o)
            self.readers[k] = {}
        return tok

    def dop(self, eng, fn, reads=(), writes=()):
        waits = self._collect(eng, reads, writes)
        sem = f"d_{eng}_{self.dpool_i[eng] % NDPOOL}"
        self.dpool_i[eng] += 1
        prev = self.dcount.get(sem, 0)
        if prev > 0 and waits.get(sem, 0) < prev:
            waits[sem] = prev
        self.dcount[sem] = prev + 16
        tok = (sem, prev + 16)
        self.ordc[eng] += 1
        o = self.ordc[eng]
        self.ops.append((eng, fn, waits, tok, True))
        for k in reads:
            self.readers.setdefault(k, {})[eng] = (tok, True, o)
        for k in writes:
            self.last_w[k] = (tok, eng, True, o)
            self.readers[k] = {}
        return tok


def build_program():
    nc = bass.Bass("TRN2", target_bir_lowering=False)

    def inp(name, shape, dt=F32):
        return nc.dram_tensor(name, list(shape), dt, kind="ExternalInput").ap()

    x_full = inp("x_full", [T, D])
    Win = inp("Win", [D, TOTAL])
    Wout = inp("Wout", [D, D])
    theta_s = inp("theta_s", [128, KH])
    score_col = inp("score_col", [KH, 1])
    slopes_col = inp("slopes_col", [KH, 1])
    ramp32 = inp("ramp32", [KH, T])
    wconv_fm = inp("wconv_fm", [128, 4 * 17])
    bconv_fm = inp("bconv_fm", [128, 17])
    deriv = inp("deriv", [1, 3])
    Wre_m = inp("Wre_m", [128, 32])
    Wim_m = inp("Wim_m", [128, 32])
    ns_re = inp("ns_re", [128, 1])
    ns_im = inp("ns_im", [128, 1])
    router = inp("router", [D, E])
    w0 = inp("w0", [E, D, HF])
    w1 = inp("w1", [E, D, HF])
    wo = inp("wo", [E, HF, D])
    ident = inp("ident", [128, 128])
    ident16 = inp("ident16", [128, 128], F16)
    ones1 = inp("ones1", [1, 128])
    emat = inp("emat", [KH, 8 * 128], BF16)
    bones = inp("bones", [128, 8 * KH], BF16)
    erow = inp("erow", [128, E])
    cst = inp("cst", [128, 4])

    out_hbm = nc.dram_tensor("out", [TO, D], F32, kind="ExternalOutput").ap()
    dbg_out = {}
    for name in DBG:
        dbg_out[name] = nc.dram_tensor("dbg_" + name, [128, 4096], F32,
                                       kind="ExternalOutput").ap()

    xbf_hbm = nc.dram_tensor("xbf_hbm", [T, D], BF16, kind="Internal").ap()
    ei_bf = nc.dram_tensor("ei_bf", [NSLOT, D], BF16, kind="Internal").ap()
    eo_hbm = nc.dram_tensor("eo_hbm", [NSLOT, D], BF16, kind="Internal").ap()
    idx_dram = nc.dram_tensor("idx_dram", [2, TO], F32, kind="Internal").ap()

    es = ExitStack()
    def sb(name, shape, dt=F32):
        return es.enter_context(nc.sbuf_tensor(name, list(shape), dt))

    c_theta = sb("c_theta", [128, KH]); c_thq = sb("c_thq", [128, KH])
    c_score = sb("c_score", [KH, 1]); c_slopes = sb("c_slopes", [KH, 1])
    lncol = sb("lncol", [128, 16])
    c_negsd = lncol[0:KH, 7:8]
    c_wconv = sb("c_wconv", [128, 68]); c_bconv = sb("c_bconv", [128, 17])
    c_derivt = sb("c_derivt", [128, 8]); c_deriv = c_derivt[0:1, 0:8]
    c_wcols = sb("c_wcols", [128, 4])
    c_wre = sb("c_wre", [128, 32]); c_wim = sb("c_wim", [128, 32])
    c_nsre = sb("c_nsre", [128, 1]); c_nsim = sb("c_nsim", [128, 1])
    c_ident = sb("c_ident", [128, 128]); c_ones = sb("c_ones", [1, 128])
    c_ident16 = sb("c_ident16", [128, 128], F16)
    c_emat = sb("c_emat", [KH, 8 * 128], BF16)
    c_bones = sb("c_bones", [128, 8 * KH], BF16)
    c_erow = sb("c_erow", [128, E]); c_cst = sb("c_cst", [128, 4])
    c_router = sb("c_router", [128, 8 * E], F32R)
    blk = sb("blk", [128, 8 * 128], BF16)
    wre_f = sb("wre_f", [128, 64]); wre_b = sb("wre_b", [128, 64], BF16)

    AR1 = sb("AR1", [128, 32768], BF16)
    AR2 = sb("AR2", [128, 34816], BF16)
    AB_ = sb("AB", [128, 33024], BF16)

    xT = AR1
    x1 = AR1[:, 0:32768].bitcast(F32)

    ugate = AR2[:, 0:16384]
    zer = AR2[:, 0:2048]
    zc = AR2[:, 16384:20480]
    s1b = AR2[:, 20480:22528]
    scale_bf_v = s1b[0:KH, :]
    s2b = AR2[:, 22528:24576]
    r1b = AR2[:, 24576:26624]
    r2b = AR2[:, 26624:28672]
    cs1 = r1b
    cs2 = r2b
    ppw = AR2[:, 28672:30720]
    pwe = AR2[:, 30720:34816]
    sexp = AR2[:, 30720:32768]
    wout_bf = AR2[:, 16384:24576]
    wob = AR2[:, 0:16384]
    hidb = AR2[:, 16384:24576]
    gelb = AR2[:, 24576:25088]
    gth = AR2[:, 0:32768]
    x1bf = AR2[:, 16384:32768]

    zbuf = AB_[:, 0:4102]
    arn1 = AB_[:, 4352:12544].bitcast(F32)
    arn2 = AB_[:, 12544:16640].bitcast(F32)
    invd_o = AB_[:, 16640:20736].bitcast(F32)[0:KH, :]
    pwb = AB_[:, 20736:24832][0:KH, :]
    mscol = arn1[0:KH, 0:TO]
    iv2c = arn1[0:KH, TO:T]
    scl32 = arn2[0:KH, 0:TO]
    sraw_v = AR2[:, 2048:10240].bitcast(F32)[0:KH, :]
    ramp_v = AR2[:, 10240:18432].bitcast(F32)[0:KH, :]
    den_v = AR2[:, 18432:26624].bitcast(F32)[0:KH, :]
    scrA = AR2[:, 26624:34816].bitcast(F32)[0:KH, :]
    xtile = AB_[:, 0:4096].bitcast(F32)
    lnc = AB_[:, 4352:6400].bitcast(F32)
    lnc2 = AB_[:, 0:2048].bitcast(F32)
    m1T = AB_[:, 4352:6400].bitcast(F16)[0:E, :]
    m2T = AB_[:, 6400:8448].bitcast(F16)[0:E, :]
    p1T = AB_[:, 8448:10496].bitcast(F16)[0:E, :]
    p2T = AB_[:, 10496:12544].bitcast(F16)[0:E, :]
    k1T = AB_[:, 12544:14592].bitcast(F16)[0:E, :]
    k2T = AB_[:, 14592:16640].bitcast(F16)[0:E, :]
    eosb = AB_[:, 0:8192].bitcast(F32)
    eobf = AB_[:, 8192:12288]
    eib = AB_[:, 12288:16384]
    w0b = AB_[:, 16384:24576]
    w1b = AR2[:, 25088:33280]
    gelb2 = AR2[:, 33280:33792]

    carry = sb("carry", [128, 8])
    eib2 = AB_[:, 26880:30976]
    sqb = AB_[:, 30976:33024]
    wbuf = AB_[:, 24832:26880]
    vcols = sb("vcols", [128, 16])
    cntT = vcols[0:E, 14:16]
    glog = sb("glog", [128, 16 * E])
    x1t = sb("x1t", [128, 128], F32R)
    g8a = sb("g8a", [128, E]); g8b = sb("g8b", [128, E]); g8c = sb("g8c", [128, E])
    m1tm = sb("m1tm", [128, 16 * E]); m2tm = sb("m2tm", [128, 16 * E])
    gcols = sb("gcols", [128, 16 * 8])
    slotff = sb("slotff", [128, 32], F32)
    sloti = sb("sloti", [128, 2 * 128], I16)
    stb32 = AB_[:, 0:256].bitcast(F32)[0:32, :]
    slotbt = AB_[:, 256:320].bitcast(F32)
    bi16c = AB_[:, 320:448].bitcast(I16)[0:32, :]

    ps = [es.enter_context(nc.psum_tensor(f"ps{i}", [128, 512], F32)) for i in range(8)]
    PK = [f"ps{i}" for i in range(8)]
    ps2h = ps[2][:, 0:4].bitcast(F16)

    sems = {}
    for e in ENGS:
        sems["c_" + e] = es.enter_context(nc.semaphore("c_" + e))
        for i in range(NDPOOL):
            sems[f"d_{e}_{i}"] = es.enter_context(nc.semaphore(f"d_{e}_{i}"))

    sch = Sched()
    O, DO = sch.op, sch.dop

    def dma(eng, out, in_, reads, writes, **kw):
        return DO(eng, lambda h: h.dma_start(out=out, in_=in_, **kw), reads, writes)

    def mm(lhsT, rhs, out, start, stop, reads, writes):
        return O("tensor",
                 lambda h: h.matmul(out, lhsT, rhs, start=start, stop=stop),
                 reads, writes)

    def trans(out, in_, reads, writes):
        p_ = in_.shape[0]
        idt = c_ident16 if in_.dtype == F16 else c_ident
        return O("tensor", lambda h: h.transpose(out, in_, idt[0:p_, 0:p_]),
                 tuple(reads) + ("c_ident",), writes)

    def v(fn, reads, writes):
        return O("vector", fn, reads, writes)

    def s(fn, reads, writes):
        return O("scalar", fn, reads, writes)

    def dbg(name, ap, reads):
        if name in DBG:
            w_ = min(ap.shape[-1], 4096)
            p_ = ap.shape[0]
            def fn(h):
                return h.dma_start(out=dbg_out[name][0:p_, 0:w_], in_=ap[:, 0:w_],
                                   allow_slow_non_contiguous=True)
            DO("gpsimd", fn, tuple(reads), tuple(reads))

    outtoks_g = []

    O("gpsimd", lambda h: h.load_library(mlp), (), ())

    for o, i, k in [
        (c_theta[:], theta_s, "c_theta"), (c_score[:], score_col, "c_score"),
        (c_slopes[:], slopes_col, "c_slopes"), (c_wconv[:], wconv_fm, "c_wconv"),
        (c_bconv[:], bconv_fm, "c_bconv"), (c_deriv[0:1, 0:3], deriv, "c_deriv"),
        (c_wre[:], Wre_m, "c_wre"), (c_wim[:], Wim_m, "c_wim"),
        (c_nsre[:], ns_re, "c_nsre"), (c_nsim[:], ns_im, "c_nsim"),
        (c_ident[:], ident, "c_ident"), (c_ones[:], ones1, "c_ones"),
        (c_ident16[:], ident16, "c_ident"),
        (c_emat[:], emat, "c_emat"), (c_bones[:], bones, "c_bones"),
        (c_erow[:], erow, "c_erow"), (c_cst[:], cst, "c_cst"),
        (ramp_v, ramp32, "ramp"),
    ]:
        dma("sync", o, i, (), (k,))
    dma("sync", c_router[:].rearrange("p (c e) -> p c e", e=E),
        router.rearrange("(c p) e -> p c e", p=128).bitcast(F32R), (), ("c_router",))

    nc.const_aps.register(F32, PI / 2, c_cst[:, 0:1])
    nc.const_aps.register(F32, TWO_PI, c_cst[:, 1:2])
    nc.const_aps.register(F32, -TWO_PI, c_cst[:, 2:3])

    v(lambda h: h.tensor_scalar_mul(c_thq[:], c_theta[:], 1.0 / TWO_PI),
      ("c_theta",), ("c_thq",))
    s(lambda h: h.activation(lncol[0:KH, 0:1], c_slopes[:], AF.Abs),
      ("c_slopes",), ("sp0",))
    v(lambda h: h.tensor_scalar_mul(lncol[0:KH, 1:2], lncol[0:KH, 0:1], -1.0),
      ("sp0",), ("sp1",))
    s(lambda h: h.activation(lncol[0:KH, 2:3], lncol[0:KH, 1:2], AF.Exp),
      ("sp1",), ("sp2",))
    v(lambda h: h.tensor_scalar(lncol[0:KH, 3:4], lncol[0:KH, 2:3], 1.0, None,
                                ALU.add), ("sp2",), ("sp3",))
    s(lambda h: h.activation(lncol[0:KH, 4:5], lncol[0:KH, 3:4], AF.Ln),
      ("sp3",), ("sp4",))
    v(lambda h: h.tensor_scalar(lncol[0:KH, 5:6], c_slopes[:], 0.0, None,
                                ALU.max), ("c_slopes",), ("sp5",))
    v(lambda h: h.tensor_scalar(lncol[0:KH, 6:7], lncol[0:KH, 4:5],
                                lncol[0:KH, 5:6], None, ALU.add),
      ("sp4", "sp5"), ("sp6",))
    v(lambda h: h.tensor_scalar_mul(c_negsd, lncol[0:KH, 6:7], -1.0),
      ("sp6",), ("c_negsd",))
    v(lambda h: h.reduce_max(c_deriv[0:1, 3:4], c_deriv[0:1, 0:3], axis=AX.X,
                             negate=True), ("c_deriv",), ("dmx",))
    s(lambda h: h.activation(c_deriv[0:1, 0:3], c_deriv[0:1, 0:3], AF.Exp,
                             bias=c_deriv[0:1, 3:4], scale=1.0),
      ("c_deriv", "dmx"), ("dex",))
    v(lambda h: h.reduce_sum(c_deriv[0:1, 3:4], c_deriv[0:1, 0:3], axis=AX.X),
      ("dex",), ("dsum",))
    v(lambda h: h.reciprocal(c_deriv[0:1, 3:4], c_deriv[0:1, 3:4]),
      ("dsum",), ("drec",))
    v(lambda h: h.tensor_scalar_mul(c_deriv[0:1, 0:3], c_deriv[0:1, 0:3],
                                    c_deriv[0:1, 3:4]), ("drec", "dex"), ("dw",))
    mm(c_ones[:], c_deriv[0:1, 0:3], ps[2][:, 0:3], True, True,
       ("dw", "c_ones"), ("ps2",))
    v(lambda h: h.tensor_copy(c_wcols[:, 0:3], ps[2][:, 0:3]), ("ps2",), ("wc",))
    v(lambda h: h.tensor_scalar_mul(c_wcols[:, 3:4], c_wcols[:, 2:3], -1.0),
      ("wc",), ("wcn",))
    v(lambda h: h.tensor_scalar_mul(wre_f[:, 0:32], c_wre[:], c_nsre[:, 0:1]),
      ("c_wre", "c_nsre"), ("wrf",))
    v(lambda h: h.tensor_scalar_mul(wre_f[:, 32:64], c_wim[:], c_nsim[:, 0:1]),
      ("c_wim", "c_nsim"), ("wrf2",))
    v(lambda h: h.tensor_copy(wre_b[:], wre_f[:]), ("wrf", "wrf2"), ("wrb",))
    v(lambda h: h.memset(blk[:], 0.0), (), ("blk",))
    for m in range(MT):
        for ri in range(2):
            for r in range(4):
                def fn(h, m=m, ri=ri, r=r):
                    return h.dma_start(
                        out=blk[32 * r:32 * r + 32,
                                (m * 2 + ri) * 128 + 32 * r:
                                (m * 2 + ri) * 128 + 32 * r + 32],
                        in_=wre_b[m * 32:(m + 1) * 32, ri * 32:ri * 32 + 32])
                DO("sync", fn, ("wrb",), ("blk",))


    dma("gpsimd", xbf_hbm, x_full, (), ("xbf",), max_dma_last_dim=4096)
    for dc in range(8):
        def fn(h, dc=dc):
            return h.dma_start(out=xT[:, dc * T:(dc + 1) * T],
                               in_=xbf_hbm[:, dc * 128:(dc + 1) * 128],
                               transpose=True)
        DO("sync", fn, ("xbf",), (f"xT{dc}",))
    XTK = tuple(f"xT{dc}" for dc in range(8))

    v(lambda h: h.memset(zer, 0.0), (), ("zer",))
    zerf = zer.bitcast(F32)
    for i in range(32):
        def fn(h, i=i):
            return h.dma_start(out=ei_bf[i * 128:(i + 1) * 128, :],
                               in_=zer[:, 0:D])
        DO("sync", fn, ("zer",), ("ei_z",))
    dma("sync", eo_hbm[E * CAP:NSLOT, :], zer[0:1, 0:D], ("zer",), ("eo_z",))

    def win_load(n, cc):
        w = 128 if cc < 16 else 32
        slot = n % 2
        def fn(h):
            return h.dma_start(
                out=wbuf[:, slot * 1024:slot * 1024 + 8 * w].rearrange(
                    "p (dc c) -> p dc c", dc=8),
                in_=Win[:, cc * 128:cc * 128 + w].rearrange("(dc p) c -> p dc c", p=128),
                max_dma_last_dim=4096)
        DO("gpsimd", fn, (), (f"wbuf{slot}",))

    def zmm(n, cc, nrow):
        w = 128 if cc < 16 else 32
        slot = n % 2
        for ts_ in range(8):
            bank = ts_ % 2
            for dc in range(8):
                mm(wbuf[:, slot * 1024 + dc * w: slot * 1024 + (dc + 1) * w],
                   xT[:, dc * T + ts_ * 512: dc * T + (ts_ + 1) * 512],
                   ps[bank][0:w, 0:512], dc == 0, dc == 7,
                   (f"wbuf{slot}",) + XTK, (PK[bank],))
            def fn(h, ts_=ts_, bank=bank, w=w):
                return h.tensor_copy(zbuf[0:w, 3 + ts_ * 512: 3 + (ts_ + 1) * 512],
                                     ps[bank][0:w, 0:512])
            v(fn, (PK[bank],), ("zbuf",))

    def conv_emit(w, ccol, out_ap, out_key):
        v(lambda h: h.tensor_scalar(arn1[0:w, :], zbuf[0:w, 0:T],
                                    c_wconv[0:w, ccol * 4:ccol * 4 + 1],
                                    c_bconv[0:w, ccol:ccol + 1], ALU.mult, ALU.add),
          ("zbuf", "c_wconv", "c_bconv"), ("arn1",))
        for tap in (1, 2):
            def fn(h, tap=tap):
                return h.scalar_tensor_tensor(
                    arn1[0:w, :], zbuf[0:w, tap:tap + T],
                    c_wconv[0:w, ccol * 4 + tap:ccol * 4 + tap + 1], arn1[0:w, :],
                    ALU.mult, ALU.add)
            v(fn, ("zbuf", "arn1"), ("arn1",))
        def fn(h):
            return h.scalar_tensor_tensor(
                out_ap, zbuf[0:w, 3:3 + T],
                c_wconv[0:w, ccol * 4 + 3:ccol * 4 + 4], arn1[0:w, :],
                ALU.mult, ALU.add)
        v(fn, ("zbuf", "arn1"), (out_key,))

    if STOP_PHASE < 1:
        return _finish(nc, es, sch, sems, outtoks_g, ENGS)
    v(lambda h: h.memset(zbuf[:, 0:3], 0.0), (), ("zbuf",))
    win_load(0, 16)
    zmm(0, 16, 32)
    conv_emit(32, 16, sraw_v, "sraw")
    dbg("sraw0", sraw_v, ("sraw",))
    v(lambda h: h.tensor_scalar(sraw_v, sraw_v, c_score[0:KH, 0:1], 20.0,
                                ALU.mult, ALU.min), ("sraw", "c_score"), ("sraw",))
    v(lambda h: h.tensor_scalar(sraw_v, sraw_v, -20.0, None, ALU.max),
      ("sraw",), ("sraw",))
    v(lambda h: h.scalar_tensor_tensor(sraw_v, ramp_v, c_negsd,
                                       sraw_v, ALU.mult, ALU.add),
      ("ramp", "c_negsd", "sraw"), ("sraw",))
    dbg("negsd", c_negsd, ("c_negsd",))
    dbg("slopes", c_slopes, ("c_slopes",))
    dbg("lncol", lncol, ("sp6", "sp4", "sp5", "sp2"))
    dbg("ramp", ramp_v, ("ramp",))
    dbg("arg", sraw_v, ("sraw",))
    s(lambda h: h.activation(pwb, sraw_v, AF.Exp), ("sraw",), ("pw",))
    dbg("pw", pwb, ("pw",))
    v(lambda h: h.tensor_tensor_scan(den_v, pwb, pwb, 0.0, ALU.add, ALU.bypass),
      ("pw",), ("den",))
    v(lambda h: h.tensor_scalar(scrA[:, 0:TO], den_v[:, HH:T], 1e-4, None,
                                ALU.max), ("den",), ("scrA",))
    v(lambda h: h.reciprocal(invd_o, scrA[:, 0:TO]), ("scrA",), ("invd",))

    if STOP_PHASE < 2:
        return _finish(nc, es, sch, sems, outtoks_g, ENGS)
    for n in range(1, 9):
        cc = n + 7
        j = cc - 8
        win_load(n, cc)
        zmm(n, cc, 128)
        conv_emit(128, cc, zc[:, 0:T], "zc")
        def fn(h, j=j):
            return h.activation(ugate[:, j * TO:(j + 1) * TO], zc[:, HH:T], AF.Silu)
        s(fn, ("zc",), (f"gate{j}",) + (("zer",) if j == 0 else ()))

    if STOP_PHASE < 3:
        return _finish(nc, es, sch, sems, outtoks_g, ENGS)
    A = arn1[:, 0:HH]
    Bv = arn1[:, HH:T]
    for n in range(9, 17):
        j = n - 9
        win_load(n, j)
        zmm(n, j, 128)
        conv_emit(128, j, zc[:, 0:T], "zc")
        if j == 0:
            dbg("zc0", zc[:, 0:2048].bitcast(BF16), ("zc",))
        for ts_ in range(8):
            mm(c_emat[:, j * 128:(j + 1) * 128],
               pwb[:, ts_ * 512:(ts_ + 1) * 512],
               ps[2][:, 0:512], True, True, ("c_emat", "pw"), ("ps2",))
            def fn(h, ts_=ts_):
                return h.activation(pwe[:, ts_ * 512:(ts_ + 1) * 512],
                                    ps[2][:, 0:512], AF.Copy)
            s(fn, ("ps2",), ("pwe",))
        for hh in range(2):
            sl = slice(hh * HH, (hh + 1) * HH)
            def fn(h, sl=sl):
                return h.tensor_scalar(A, zc[:, sl], c_wcols[:, 3:4],
                                       c_wcols[:, 1:2], ALU.mult, ALU.add)
            v(fn, ("zc", "wcn", "wc"), ("arn1",))
            def fn(h, sl=sl):
                return h.tensor_tensor(arn2[:, :], A, zc[:, sl], ALU.mult)
            v(fn, ("arn1", "zc"), ("arn2",))
            def fn(h, sl=sl):
                return h.scalar_tensor_tensor(ppw[:, :], arn2[:, :], c_wcols[:, 0:1],
                                              pwe[:, sl], ALU.add, ALU.mult)
            v(fn, ("arn2", "wc", "pwe"), ("ppw",))
            if hh == 1:
                tails = []
            for m in range(MT):
                tc = j * 4 + m
                def fn(h, sl=sl, tc=tc):
                    return h.activation(A, zc[:, sl], AF.Copy,
                                        bias=0.0, scale=c_thq[:, tc:tc + 1])
                s(fn, ("zc", "c_thq"), ("arn1",))
                s(lambda h: h.activation(Bv.bitcast(I32), A, AF.Copy),
                  ("arn1",), ("arn1",))
                s(lambda h: h.activation(arn2[:, :], Bv.bitcast(I32), AF.Copy),
                  ("arn1",), ("arn2",))
                v(lambda h: h.scalar_tensor_tensor(Bv, arn2[:, :], -1.0, A,
                                                   ALU.mult, ALU.add),
                  ("arn2", "arn1"), ("arn1",))
                if hh == 1:
                    for t_ in tails:
                        t_()
                    tails = []
                s(lambda h: h.activation(s1b[:, :], Bv, AF.Sin, bias=0.0,
                                         scale=TWO_PI), ("arn1",), ("s1b",))
                s(lambda h: h.activation(arn2[:, :], Bv, AF.Abs), ("arn1",), ("arn2",))
                s(lambda h: h.activation(s2b[:, :], arn2[:, :], AF.Sin,
                                         bias=PI / 2, scale=-TWO_PI),
                  ("arn2",), ("s2b",))
                cm = (f"cr{m}", f"ci{m}")
                if hh == 0:
                    def fn(h, m=m):
                        return h.scalar_tensor_tensor(
                            r1b[:, :], ppw[:, :], 1.0, s2b[:, :],
                            ALU.mult, ALU.mult,
                            accum_out=carry[:, 2 * m:2 * m + 1])
                    v(fn, ("ppw", "s2b"), (cm[0], "r1b"))
                    if j == 0 and m == 0:
                        dbg("mgc", r1b, ("r1b",))
                        dbg("uu", A, ("arn1",))
                        dbg("rr", Bv, ("arn1",))
                        dbg("sinv", s1b, ("s1b",))
                        dbg("cosv", s2b, ("s2b",))
                        dbg("ppwv", ppw, ("ppw",))
                        dbg("wcols", c_wcols, ("wc", "wcn"))
                        dbg("pwev", pwe, ("pwe",))
                        dbg("pwbv", pwb, ("pw",))
                        dbg("carry0", carry, (cm[0],))
                    def fn(h, m=m):
                        return h.scalar_tensor_tensor(
                            r2b[:, :], ppw[:, :], 1.0, s1b[:, :],
                            ALU.mult, ALU.mult,
                            accum_out=carry[:, 2 * m + 1:2 * m + 2])
                    v(fn, ("ppw", "s1b"), (cm[1], "r2b"))
                else:
                    v(lambda h: h.tensor_tensor(r1b[:, :], ppw[:, :], s2b[:, :],
                                                ALU.mult), ("ppw", "s2b"), ("r1b",))
                    v(lambda h: h.tensor_tensor(r2b[:, :], ppw[:, :], s1b[:, :],
                                                ALU.mult), ("ppw", "s1b"), ("r2b",))
                    def fn(h, m=m):
                        return h.tensor_tensor_scan(cs1[:, :], r1b[:, :], r1b[:, :],
                                                    carry[:, 2 * m:2 * m + 1],
                                                    ALU.add, ALU.bypass)
                    v(fn, ("r1b", cm[0]), ("r1b",))
                    def fn(h, m=m):
                        return h.tensor_tensor_scan(cs2[:, :], r2b[:, :], r2b[:, :],
                                                    carry[:, 2 * m + 1:2 * m + 2],
                                                    ALU.add, ALU.bypass)
                    v(fn, ("r2b", cm[1]), ("r2b",))
                    if j == 0 and m == 0:
                        dbg("cs00", cs1, ("r1b",))
                        dbg("ppw0", ppw, ("ppw",))
                        dbg("s2b0", s2b, ("s2b",))
                        dbg("u0arg", A, ("arn1",))
                        dbg("pwe0", pwe, ("pwe",))
                        dbg("poly0", arn2, ("arn2",))
                    for ri in range(2):
                        csb, ck = (cs1, "r1b") if ri == 0 else (cs2, "r2b")
                        first = (m == 0 and ri == 0)
                        last = (m == 3 and ri == 1)
                        bcol = (m * 2 + ri) * 128
                        for ts_ in range(4):
                            mm(blk[:, bcol:bcol + 128],
                               csb[:, ts_ * 512:(ts_ + 1) * 512],
                               ps[ts_][:, 0:512], first, last,
                               ("blk", ck), (PK[ts_],))
                        def tail(csb=csb, ck=ck, first=first, last=last, j=j):
                            def fn(h, csb=csb):
                                return h.tensor_tensor(sqb[:, :], csb[:], csb[:],
                                                       ALU.mult)
                            v(fn, (ck,), ("sqb",))
                            for ts_ in range(4):
                                mm(c_bones[:, j * KH:(j + 1) * KH],
                                   sqb[:, ts_ * 512:(ts_ + 1) * 512],
                                   ps[4 + ts_][0:KH, 0:512], first, last,
                                   ("c_bones", "sqb"), (PK[4 + ts_],))
                        tails.append(tail)
            if hh == 1:
                for t_ in tails:
                    t_()
                tails = []
        for ts_ in range(4):
            def fn(h, ts_=ts_):
                return h.activation(mscol[:, ts_ * 512:(ts_ + 1) * 512],
                                    ps[4 + ts_][0:KH, 0:512], AF.Copy)
            s(fn, (PK[4 + ts_],), ("arn1",))
        v(lambda h: h.tensor_tensor(iv2c, invd_o, invd_o, ALU.mult),
          ("invd",), ("arn1",))
        v(lambda h: h.tensor_tensor(scl32, mscol, iv2c, ALU.mult),
          ("arn1",), ("arn2",))
        v(lambda h: h.tensor_scalar(scl32, scl32, 1.0 / 256.0, 1e-5,
                                    ALU.mult, ALU.add), ("arn2",), ("arn2",))
        s(lambda h: h.activation(iv2c, scl32, AF.Sqrt), ("arn2",), ("arn1",))
        v(lambda h: h.reciprocal(mscol, iv2c), ("arn1",), ("arn1",))
        v(lambda h: h.tensor_tensor(scl32, invd_o, mscol, ALU.mult),
          ("invd", "arn1"), ("arn2",))
        v(lambda h: h.tensor_copy(scale_bf_v, scl32), ("arn2",), ("s1b",))
        for ts_ in range(4):
            mm(c_emat[:, j * 128:(j + 1) * 128],
               scale_bf_v[:, ts_ * 512:(ts_ + 1) * 512],
               ps[4 + ts_][:, 0:512], True, True,
               ("c_emat", "s1b"), (PK[4 + ts_],))
            def fn(h, ts_=ts_):
                return h.activation(sexp[:, ts_ * 512:(ts_ + 1) * 512],
                                    ps[4 + ts_][:, 0:512], AF.Copy)
            s(fn, (PK[4 + ts_],), ("pwe",))
        for ts_ in range(4):
            def fn(h, ts_=ts_):
                return h.tensor_tensor(s1b[:, ts_ * 512:(ts_ + 1) * 512],
                                       ps[ts_][:, 0:512],
                                       sexp[:, ts_ * 512:(ts_ + 1) * 512], ALU.mult)
            v(fn, (PK[ts_], "sexp"), ("s1b",))
        def fn(h, j=j):
            return h.tensor_tensor(ugate[:, j * TO:(j + 1) * TO], s1b[:, :],
                                   ugate[:, j * TO:(j + 1) * TO], ALU.mult)
        v(fn, ("s1b", f"gate{j}"), (f"u{j}",))
        if j == 0:
            dbg("u0", ugate[:, 0:TO], ("u0",))

    if STOP_PHASE < 5:
        return _finish(nc, es, sch, sems, outtoks_g, ENGS)
    def fn(h):
        return h.dma_start(out=wout_bf[:, :].rearrange("p (c d) -> p c d", c=8),
                           in_=Wout.rearrange("(c p) d -> p c d", p=128),
                           max_dma_last_dim=4096)
    DO("gpsimd", fn, ("zc", "s1b", "s2b", "r1b"),
       ("woutb", "zc", "s1b", "s2b", "r1b"))
    ureads = tuple(f"u{j}" for j in range(8))
    for i in range(16):
        def fn(h, i=i):
            return h.dma_start(
                out=xtile[:, (i % 2) * D:(i % 2 + 1) * D],
                in_=x_full.rearrange("(g q) m -> q g m", q=128)[:, 16 + i, :])
        DO("sync", fn, (), (f"xt{i % 2}",) + (("zbuf",) if i < 2 else ()))
        for half in range(2):
            for cc2 in range(8):
                mm(ugate[:, cc2 * TO + i * 128:cc2 * TO + (i + 1) * 128],
                   wout_bf[:, cc2 * D + half * 512:cc2 * D + (half + 1) * 512],
                   ps[half][:, 0:512], cc2 == 0, cc2 == 7,
                   ureads + ("woutb",), (PK[half],))
        for half in range(2):
            def fn(h, i=i, half=half):
                return h.tensor_tensor(
                    lnc[:, half * 512:(half + 1) * 512], ps[half][:, 0:512],
                    xtile[:, (i % 2) * D + half * 512:(i % 2) * D + (half + 1) * 512],
                    ALU.add)
            v(fn, (PK[half], f"xt{i % 2}"), ("lnc",))
        if i == 0:
            dbg("xr0", lnc, ("lnc",))
        v(lambda h: h.reduce_sum(lncol[:, 0:1], lnc[:, :], axis=AX.X),
          ("lnc",), ("ln0",))
        v(lambda h: h.tensor_scalar_mul(lncol[:, 1:2], lncol[:, 0:1], 1.0 / D),
          ("ln0",), ("ln1",))
        def fn(h, i=i):
            return h.tensor_scalar(x1[:, i * D:(i + 1) * D], lnc[:, :],
                                   lncol[:, 1:2], None, ALU.subtract)
        v(fn, ("lnc", "ln1"), (f"x1c{i}",) + (("zer",) if i < 4 else ()))
        def fn(h, i=i):
            return h.tensor_tensor(lnc[:, :], x1[:, i * D:(i + 1) * D],
                                   x1[:, i * D:(i + 1) * D], ALU.mult)
        v(fn, (f"x1c{i}",), ("lnc",))
        def fn(h, i=i):
            return h.reduce_sum(vcols[:, i:i + 1], lnc[:, :], axis=AX.X)
        v(fn, ("lnc",), (f"vc{i}",))
    vreads = tuple(f"vc{i}" for i in range(16))
    v(lambda h: h.tensor_scalar(vcols[:], vcols[:], 1.0 / D, 1e-5,
                                ALU.mult, ALU.add), vreads, ("vca",))
    s(lambda h: h.activation(vcols[:], vcols[:], AF.Sqrt), ("vca",), ("vcs",))
    v(lambda h: h.reciprocal(vcols[:], vcols[:]), ("vcs",), ("vcr",))
    for i in range(16):
        def fn(h, i=i):
            return h.tensor_scalar_mul(x1[:, i * D:(i + 1) * D],
                                       x1[:, i * D:(i + 1) * D], vcols[:, i:i + 1])
        v(fn, ("vcr", f"x1c{i}"), (f"x1_{i}",))
    dbg("x1", x1[:, 0:4096], tuple(f"x1_{i}" for i in range(4)))

    if STOP_PHASE < 6:
        return _finish(nc, es, sch, sems, outtoks_g, ENGS)
    for i in range(16):
        for mc in range(8):
            trans(ps[2][:, 0:128], x1[:, i * D + mc * 128:i * D + (mc + 1) * 128],
                  (f"x1_{i}",), ("ps2",))
            v(lambda h: h.tensor_copy(x1t[:], ps[2][:, 0:128]), ("ps2",), ("x1t",))
            mm(x1t[:], c_router[:, mc * E:(mc + 1) * E],
               ps[3][:, 0:E], mc == 0, mc == 7, ("x1t", "c_router"), ("ps3",))
        def fn(h, i=i):
            return h.tensor_copy(glog[:, i * E:(i + 1) * E], ps[3][:, 0:E])
        v(fn, ("ps3",), (f"lg{i}",))
    dbg("glog", glog[:], tuple(f"lg{i}" for i in range(16)))

    if STOP_PHASE < 7:
        return _finish(nc, es, sch, sems, outtoks_g, ENGS)
    for i in range(16):
        lg = glog[:, i * E:(i + 1) * E]
        raw = m2tm[:, i * E:(i + 1) * E]
        def fn(h, lg=lg):
            return h.reduce_max(lncol[:, 2:3], lg, axis=AX.X, negate=True)
        v(fn, (f"lg{i}",), ("gnm",))
        def fn(h, lg=lg):
            return h.activation(g8a[:], lg, AF.Exp, bias=lncol[:, 2:3], scale=1.0)
        s(fn, (f"lg{i}", "gnm"), ("g8a",))
        v(lambda h: h.reduce_sum(lncol[:, 3:4], g8a[:], axis=AX.X),
          ("g8a",), ("gss",))
        v(lambda h: h.reciprocal(lncol[:, 3:4], lncol[:, 3:4]), ("gss",), ("gsr",))
        def fn(h, raw=raw):
            return h.tensor_scalar_mul(raw, g8a[:], lncol[:, 3:4])
        v(fn, ("g8a", "gsr"), ("raw",))
        def fn(h, raw=raw, i=i):
            return h.reduce_max(gcols[:, i * 8 + 0:i * 8 + 1], raw, axis=AX.X)
        v(fn, ("raw",), ("mx1",))
        def fn(h, raw=raw, i=i):
            return h.tensor_scalar(g8a[:], raw, gcols[:, i * 8 + 0:i * 8 + 1],
                                   None, ALU.is_ge)
        v(fn, ("raw", "mx1"), ("eq1",))
        v(lambda h: h.tensor_tensor_scan(g8b[:], g8a[:], g8a[:], 0.0, ALU.add,
                                         ALU.bypass), ("eq1",), ("cum1",))
        v(lambda h: h.tensor_scalar(g8b[:], g8b[:], 1.5, None, ALU.is_le),
          ("cum1",), ("cl1",))
        v(lambda h: h.tensor_tensor(g8a[:], g8a[:], g8b[:], ALU.mult),
          ("eq1", "cl1"), ("m1v",))
        def fn(h, raw=raw):
            return h.scalar_tensor_tensor(g8c[:], g8a[:], -1e30, raw,
                                          ALU.mult, ALU.add)
        v(fn, ("m1v", "raw"), ("raw2",))
        def fn(h, i=i):
            return h.reduce_max(gcols[:, i * 8 + 1:i * 8 + 2], g8c[:], axis=AX.X)
        v(fn, ("raw2",), ("mx2",))
        def fn(h, i=i):
            return h.tensor_scalar(g8c[:], g8c[:], gcols[:, i * 8 + 1:i * 8 + 2],
                                   None, ALU.is_ge)
        v(fn, ("raw2", "mx2"), ("eq2",))
        v(lambda h: h.tensor_tensor_scan(g8b[:], g8c[:], g8c[:], 0.0, ALU.add,
                                         ALU.bypass), ("eq2",), ("cum2",))
        v(lambda h: h.tensor_scalar(g8b[:], g8b[:], 1.5, None, ALU.is_le),
          ("cum2",), ("cl2",))
        v(lambda h: h.tensor_tensor(g8c[:], g8c[:], g8b[:], ALU.mult),
          ("eq2", "cl2"), ("m2v",))
        v(lambda h: h.tensor_tensor(g8b[:], g8a[:], c_erow[:], ALU.mult),
          ("m1v", "c_erow"), ("i1t",))
        def fn(h, i=i):
            return h.reduce_sum(gcols[:, i * 8 + 2:i * 8 + 3], g8b[:], axis=AX.X)
        v(fn, ("i1t",), (f"i1_{i}",))
        v(lambda h: h.tensor_tensor(g8b[:], g8c[:], c_erow[:], ALU.mult),
          ("m2v", "c_erow"), ("i2t",))
        def fn(h, i=i):
            return h.reduce_sum(gcols[:, i * 8 + 3:i * 8 + 4], g8b[:], axis=AX.X)
        v(fn, ("i2t",), (f"i2_{i}",))
        def fn(h, i=i):
            return h.tensor_copy(m1tm[:, i * E:(i + 1) * E], g8a[:])
        v(fn, ("m1v",), (f"m1_{i}",))
        def fn(h, i=i):
            return h.tensor_copy(m2tm[:, i * E:(i + 1) * E], g8c[:])
        v(fn, ("m2v",), (f"m2_{i}",))
        trans(ps[2][0:E, 0:128], m1tm[:, i * E:(i + 1) * E], (f"m1_{i}",), ("ps2",))
        def fn(h, i=i):
            return h.tensor_copy(m1T[:, i * 128:(i + 1) * 128], ps[2][0:E, 0:128])
        v(fn, ("ps2",), ("m1T",))
        trans(ps[2][0:E, 0:128], m2tm[:, i * E:(i + 1) * E], (f"m2_{i}",), ("ps2",))
        def fn(h, i=i):
            return h.tensor_copy(m2T[:, i * 128:(i + 1) * 128], ps[2][0:E, 0:128])
        v(fn, ("ps2",), ("m2T",))

    if STOP_PHASE < 8:
        return _finish(nc, es, sch, sems, outtoks_g, ENGS)
    v(lambda h: h.tensor_tensor_scan(p1T[:, :], m1T[:, :], m1T[:, :], 0.0, ALU.add,
                                     ALU.bypass), ("m1T",), ("p1x",))
    v(lambda h: h.tensor_tensor(p1T[:, :], p1T[:, :], m1T[:, :], ALU.subtract),
      ("p1x", "m1T"), ("p1T",))
    v(lambda h: h.tensor_scalar(k1T[:, :], p1T[:, :], float(CAP), None, ALU.is_lt),
      ("p1T",), ("k1a",))
    v(lambda h: h.tensor_tensor(k1T[:, :], k1T[:, :], m1T[:, :], ALU.mult),
      ("k1a", "m1T"), ("k1T",))
    v(lambda h: h.reduce_sum(cntT[:, 0:1], k1T[:, :], axis=AX.X), ("k1T",), ("cnt1",))
    v(lambda h: h.tensor_tensor_scan(p2T[:, :], m2T[:, :], m2T[:, :], cntT[:, 0:1],
                                     ALU.add, ALU.bypass), ("m2T", "cnt1"), ("p2x",))
    v(lambda h: h.tensor_tensor(p2T[:, :], p2T[:, :], m2T[:, :], ALU.subtract),
      ("p2x", "m2T"), ("p2T",))
    v(lambda h: h.tensor_scalar(k2T[:, :], p2T[:, :], float(CAP), None, ALU.is_lt),
      ("p2T",), ("k2a",))
    v(lambda h: h.tensor_tensor(k2T[:, :], k2T[:, :], m2T[:, :], ALU.mult),
      ("k2a", "m2T"), ("k2T",))

    for i in range(16):
        specs = [(p1T, "p1T", "tp1"), (k1T, "k1T", "tk1"),
                 (p2T, "p2T", "tp2"), (k2T, "k2T", "tk2")]
        homes = {"tp1": g8a, "tk1": g8b, "tp2": g8c, "tk2": None}
        for (src, key, tag) in specs:
            trans(ps2h, src[:, i * 128:(i + 1) * 128], (key,), ("ps2",))
            home = homes[tag]
            if home is None:
                def fn(h, i=i):
                    return h.tensor_copy(m1tm[:, i * E:(i + 1) * E], ps2h)
                v(fn, ("ps2",), (tag,))
            else:
                def fn(h, home=home):
                    return h.tensor_copy(home[:], ps2h)
                v(fn, ("ps2",), (tag,))
        k2tm = m1tm[:, i * E:(i + 1) * E]
        v(lambda h: h.tensor_tensor(g8a[:], g8a[:], g8b[:], ALU.mult),
          ("tp1", "tk1"), ("pp1",))
        v(lambda h: h.reduce_sum(lncol[:, 4:5], g8a[:], axis=AX.X),
          ("pp1",), ("pos1",))
        v(lambda h: h.reduce_sum(lncol[:, 5:6], g8b[:], axis=AX.X),
          ("tk1",), ("k1f",))
        def fn(h, k2tm=k2tm):
            return h.tensor_tensor(g8c[:], g8c[:], k2tm, ALU.mult)
        v(fn, ("tp2", "tk2"), ("pp2",))
        v(lambda h: h.reduce_sum(lncol[:, 6:7], g8c[:], axis=AX.X),
          ("pp2",), ("pos2",))
        def fn(h, k2tm=k2tm):
            return h.reduce_sum(lncol[:, 7:8], k2tm, axis=AX.X)
        v(fn, ("tk2",), ("k2f",))
        def fn(h, i=i):
            return h.tensor_scalar_mul(gcols[:, i * 8 + 4:i * 8 + 5],
                                       gcols[:, i * 8 + 0:i * 8 + 1],
                                       lncol[:, 5:6])
        v(fn, ("mx1", "k1f"), (f"g1_{i}",))
        def fn(h, i=i):
            return h.tensor_scalar_mul(gcols[:, i * 8 + 5:i * 8 + 6],
                                       gcols[:, i * 8 + 1:i * 8 + 2],
                                       lncol[:, 7:8])
        v(fn, ("mx2", "k2f"), (f"g2_{i}",))
        for (c_i, c_p, c_k, oc, tag) in [(2, 4, 5, 6, f"s1_{i}"),
                                         (3, 6, 7, 7, f"s2_{i}")]:
            def fn(h, i=i, c_i=c_i, c_p=c_p):
                return h.tensor_scalar(
                    lncol[:, 8:9], gcols[:, i * 8 + c_i:i * 8 + c_i + 1], 512.0,
                    lncol[:, c_p:c_p + 1], ALU.mult, ALU.add)
            v(fn, (f"i1_{i}", f"i2_{i}", "pos1", "pos2"), ("sl0",))
            v(lambda h: h.tensor_scalar(lncol[:, 8:9], lncol[:, 8:9], -4096.0,
                                        None, ALU.add), ("sl0",), ("sl1",))
            def fn(h, c_k=c_k):
                return h.tensor_scalar_mul(lncol[:, 8:9], lncol[:, 8:9],
                                           lncol[:, c_k:c_k + 1])
            v(fn, ("sl1", "k1f", "k2f"), ("sl2",))
            def fn(h, i=i, oc=oc):
                return h.tensor_scalar(gcols[:, i * 8 + oc:i * 8 + oc + 1],
                                       lncol[:, 8:9], 4096.0, None, ALU.add)
            v(fn, ("sl2",), (tag,))
    dbg("gcols", gcols[:], tuple(f"s1_{i}" for i in range(16)))

    sreads = tuple(f"s1_{i}" for i in range(16)) + tuple(f"s2_{i}" for i in range(16))
    v(lambda h: h.tensor_copy(slotff[:, 0:16], gcols[:, 6:128:8]), sreads, ("slotff",))
    v(lambda h: h.tensor_copy(slotff[:, 16:32], gcols[:, 7:128:8]),
      sreads, ("slotff",))
    trans(ps[2][0:32, 0:128], slotff[:, 0:32], ("slotff",), ("ps2",))
    v(lambda h: h.tensor_copy(stb32[:, :], ps[2][0:32, 0:128]), ("ps2",), ("stb",))
    for q in range(2):
        def fn(h, q=q):
            return h.dma_start(
                out=idx_dram[q, :].rearrange("(i p) -> i p", i=16),
                in_=stb32[q * 16:(q + 1) * 16, :])
        DO("sync", fn, ("stb",), (f"idxd{q}",))
        def fn(h, q=q):
            return h.dma_start(
                out=slotbt[:, q * 16:(q + 1) * 16],
                in_=idx_dram[q, :].rearrange("(jj p2) -> jj p2", jj=128))
        DO("sync", fn, (f"idxd{q}",), ("slotbt",))
    trans(ps[2][0:32, 0:128], slotbt[:, 0:32], ("slotbt",), ("ps2",))
    v(lambda h: h.tensor_copy(bi16c[:, :], ps[2][0:32, 0:128]), ("ps2",), ("bi16",))
    for k in range(8):
        for q in range(2):
            def fn(h, k=k, q=q):
                return h.dma_start(
                    out=sloti[16 * k:16 * (k + 1), q * 128:(q + 1) * 128],
                    in_=bi16c[q * 16:(q + 1) * 16, :])
            DO("sync", fn, ("bi16",), ("sloti",))
    dbg("sloti", sloti, ("sloti",))

    v(lambda h: h.tensor_copy(x1bf[:, :], x1[:, :]),
      tuple(f"x1_{i}" for i in range(16)), ("x1bf",))
    for q in range(2):
        def fn(h, q=q):
            return h.dma_scatter_add(
                ei_bf, x1bf[:, :].rearrange("p (g m) -> p g m", g=16),
                sloti[:, q * 128:(q + 1) * 128], TO, TO, D, single_packet=False)
        DO("gpsimd", fn, ("x1bf", "sloti", "ei_z"), (f"ei_s{q}",))

    if STOP_PHASE < 9:
        return _finish(nc, es, sch, sems, outtoks_g, ENGS)
    eibs = [eib, eib2]
    gelbs = [gelb, gelb2]
    HW0 = 512
    chunks = []
    for sup in range(3):
        h0s = sup * 2048
        hns = min(2048, HF - h0s)
        for hc in range((hns + HW0 - 1) // HW0):
            hh0 = h0s + hc * HW0
            chunks.append((sup, hc, hh0, min(HW0, HF - hh0)))

    def eib_loads(e):
        eb = e % 2
        for mc in range(8):
            def fn(h, e=e, mc=mc, eb=eb):
                return h.dma_start(out=eibs[eb][:, mc * CAP:(mc + 1) * CAP],
                                   in_=ei_bf[e * CAP:(e + 1) * CAP,
                                              mc * 128:(mc + 1) * 128],
                                   transpose=True)
            DO("sync", fn, ("ei_s0", "ei_s1"), (f"eib{eb}",))

    eib_loads(0)
    for e in range(E):
        eb = e % 2
        ecur = eibs[eb]
        if e == 0:
            dbg("eib0", ecur, (f"eib{eb}",))
        for gc, (sup, hc, hh0, hw) in enumerate(chunks):
            dbs = gc % 2
            for wsrc, wdst, wkey, xk in ((w0, w0b, f"w0b{dbs}", ("x1_15",)),
                                         (w1, w1b, f"w1b{dbs}",
                                          ("ei_s0", "ei_s1"))):
                def fn(h, wsrc=wsrc, wdst=wdst, e=e, hh0=hh0, hw=hw, dbs=dbs):
                    return h.dma_start(
                        out=wdst[:, dbs * 4096:dbs * 4096 + 8 * hw].rearrange(
                            "p (mc hx) -> p mc hx", mc=8),
                        in_=wsrc[e, :, hh0:hh0 + hw].rearrange(
                            "(mc p) hx -> p mc hx", p=128),
                        max_dma_last_dim=4096)
                DO("gpsimd", fn, xk, (wkey,))
            for hs in range((hw + 127) // 128):
                w_ = min(128, hw - hs * 128)
                bk = hc * 4 + hs
                def fn(h, e=e, hh0=hh0, hs=hs, w_=w_, bk=bk):
                    return h.dma_start(
                        out=wob[0:w_, bk * D:(bk + 1) * D],
                        in_=wo[e, hh0 + hs * 128:hh0 + hs * 128 + w_, :],
                        max_dma_last_dim=4096)
                DO("gpsimd", fn, ("x1_15",), (f"wob{bk}",))
            for hs in range((hw + 127) // 128):
                w_ = min(128, hw - hs * 128)
                bk = hc * 4 + hs
                b0, b1 = (4, 5) if bk % 2 == 0 else (0, 1)
                gsl = bk % 2
                for mc in range(8):
                    mm(w0b[:, dbs * 4096 + mc * hw + hs * 128:
                           dbs * 4096 + mc * hw + hs * 128 + w_],
                       ecur[:, mc * CAP:(mc + 1) * CAP],
                       ps[b0][0:w_, 0:512], mc == 0, mc == 7,
                       (f"w0b{dbs}", f"eib{eb}"), (PK[b0],))
                    mm(w1b[:, dbs * 4096 + mc * hw + hs * 128:
                           dbs * 4096 + mc * hw + hs * 128 + w_],
                       ecur[:, mc * CAP:(mc + 1) * CAP],
                       ps[b1][0:w_, 0:512], mc == 0, mc == 7,
                       (f"w1b{dbs}", f"eib{eb}"), (PK[b1],))
                def fn(h, w_=w_, b0=b0, gsl=gsl):
                    return h.activation(gelbs[gsl][0:w_, :], ps[b0][0:w_, 0:512],
                                        AF.Gelu_apprx_tanh)
                s(fn, (PK[b0],), (f"gel{gsl}",))
                def fn(h, bk=bk, w_=w_, b1=b1, gsl=gsl):
                    return h.tensor_tensor(
                        hidb[0:w_, bk * CAP:(bk + 1) * CAP],
                        gelbs[gsl][0:w_, :], ps[b1][0:w_, 0:512], ALU.mult)
                v(fn, (f"gel{gsl}", PK[b1]), (f"hid{bk}",))
            if gc == 0 and e + 1 < E:
                eib_loads(e + 1)
            last_of_sup = (gc + 1 == len(chunks)) or (chunks[gc + 1][0] != sup)
            if not last_of_sup:
                continue
            nmm = []
            for (s2, hc2, hh02, hw2) in chunks:
                if s2 != sup:
                    continue
                for hs2 in range((hw2 + 127) // 128):
                    nmm.append((hc2 * 4 + hs2, min(128, hw2 - hs2 * 128)))
            for cm in range(8):
                cch, msl = cm // 2, cm % 2
                bank = cm % 2 + 6
                for q_, (bk, w_) in enumerate(nmm):
                    mm(hidb[0:w_, bk * CAP + cch * 128:bk * CAP + (cch + 1) * 128],
                       wob[0:w_, bk * D + msl * 512:bk * D + (msl + 1) * 512],
                       ps[bank][:, 0:512], q_ == 0, q_ == len(nmm) - 1,
                       (f"hid{bk}", f"wob{bk}"), (PK[bank],))
                if sup == 0:
                    def fn(h, cch=cch, msl=msl, bank=bank):
                        return h.tensor_copy(
                            eosb[:, cch * D + msl * 512:cch * D + (msl + 1) * 512],
                            ps[bank][:, 0:512])
                    v(fn, (PK[bank],), (f"eos{cm}",))
                elif sup == 1:
                    def fn(h, cch=cch, msl=msl, bank=bank):
                        return h.tensor_tensor(
                            eosb[:, cch * D + msl * 512:cch * D + (msl + 1) * 512],
                            ps[bank][:, 0:512],
                            eosb[:, cch * D + msl * 512:cch * D + (msl + 1) * 512],
                            ALU.add)
                    v(fn, (PK[bank], f"eos{cm}"), (f"eos{cm}",))
                else:
                    def fn(h, cch=cch, msl=msl, bank=bank):
                        return h.tensor_tensor(
                            eobf[:, cch * D + msl * 512:cch * D + (msl + 1) * 512],
                            ps[bank][:, 0:512],
                            eosb[:, cch * D + msl * 512:cch * D + (msl + 1) * 512],
                            ALU.add)
                    v(fn, (PK[bank], f"eos{cm}"), (f"eob{cm}",))
        if e == 0:
            dbg("eob0", eobf, tuple(f"eob{cm}" for cm in range(8)))
        def fn(h, e=e):
            return h.dma_start(
                out=eo_hbm[e * CAP:(e + 1) * CAP, :].rearrange(
                    "(g q) m -> q g m", q=128),
                in_=eobf[:, :].rearrange("q (g m) -> q g m", g=4))
        DO("sync", fn, tuple(f"eob{cm}" for cm in range(8)), ("eo_hbm",))

    if STOP_PHASE < 10:
        return _finish(nc, es, sch, sems, outtoks_g, ENGS)
    for q in range(2):
        def fn(h, q=q):
            return h.dma_gather(
                gth[:, q * 16 * D:(q + 1) * 16 * D].rearrange(
                    "p (g m) -> p g m", g=16),
                eo_hbm, sloti[:, q * 128:(q + 1) * 128], TO, TO, D,
                single_packet=False)
        DO("gpsimd", fn, ("eo_hbm", "eo_z", "sloti"), (f"gth{q}",))

    outtoks = outtoks_g
    for i in range(16):
        def fn(h, i=i):
            return h.tensor_scalar_mul(lnc2, gth[:, i * D:(i + 1) * D],
                                       gcols[:, i * 8 + 4:i * 8 + 5])
        v(fn, ("gth0", f"g1_{i}"),
          ("lnc",) + (("p1T", "p2T", "k1T", "k2T", "xt0", "xt1") if i == 0 else ()))
        def fn(h, i=i):
            return h.scalar_tensor_tensor(lnc2, gth[:, 16 * D + i * D:
                                                      16 * D + (i + 1) * D],
                                          gcols[:, i * 8 + 5:i * 8 + 6], lnc2,
                                          ALU.mult, ALU.add)
        v(fn, ("gth1", f"g2_{i}", "lnc"), ("lnc",))
        def fn(h, i=i):
            return h.tensor_tensor(lnc2, lnc2, x1[:, i * D:(i + 1) * D],
                                   ALU.add)
        v(fn, ("lnc", f"x1_{i}"), ("lnc",))
        v(lambda h: h.reduce_sum(lncol[:, 0:1], lnc2, axis=AX.X),
          ("lnc",), ("fln0",))
        v(lambda h: h.tensor_scalar_mul(lncol[:, 1:2], lncol[:, 0:1], 1.0 / D),
          ("fln0",), ("fln1",))
        def fn(h, i=i):
            return h.tensor_scalar(x1[:, i * D:(i + 1) * D], lnc2,
                                   lncol[:, 1:2], None, ALU.subtract)
        v(fn, ("lnc", "fln1"), (f"x2c{i}",))
        def fn(h, i=i):
            return h.tensor_tensor(lnc2, x1[:, i * D:(i + 1) * D],
                                   x1[:, i * D:(i + 1) * D], ALU.mult)
        v(fn, (f"x2c{i}",), ("lnc",))
        def fn(h, i=i):
            return h.reduce_sum(vcols[:, i:i + 1], lnc2, axis=AX.X)
        v(fn, ("lnc",), (f"wc{i}",))
    wreads = tuple(f"wc{i}" for i in range(16))
    v(lambda h: h.tensor_scalar(vcols[:], vcols[:], 1.0 / D, 1e-5,
                                ALU.mult, ALU.add), wreads, ("wca",))
    s(lambda h: h.activation(vcols[:], vcols[:], AF.Sqrt), ("wca",), ("wcs",))
    v(lambda h: h.reciprocal(vcols[:], vcols[:]), ("wcs",), ("wcr",))
    for i in range(16):
        def fn(h, i=i):
            return h.tensor_scalar_mul(x1[:, i * D:(i + 1) * D],
                                       x1[:, i * D:(i + 1) * D], vcols[:, i:i + 1])
        v(fn, ("wcr", f"x2c{i}"), (f"x2_{i}",))
        def fn(h, i=i):
            return h.dma_start(
                out=out_hbm.rearrange("(g q) m -> q g m", q=128)[:, i, :],
                in_=x1[:, i * D:(i + 1) * D])
        tok = DO("sync", fn, (f"x2_{i}",), (f"out{i}",))
        outtoks.append(tok)

    return _finish(nc, es, sch, sems, outtoks_g, ENGS)


def _finish(nc, es, sch, sems, outtoks, _engs):
    for tok in outtoks:
        sch.targets.add(tok)

    with es:
        block = es.enter_context(nc.Block())
        per_eng = {e: [] for e in ENGS}
        for rec in sch.ops:
            per_eng[rec[0]].append(rec)

        def make_emitter(eng, recs):
            def emit(h):
                known = {}
                cinc = 0
                for (_, fn, waits, tok, is_dma) in recs:
                    for sem_name, val in sorted(waits.items()):
                        if known.get(sem_name, 0) < val:
                            h.wait_ge(sems[sem_name], val)
                            known[sem_name] = val
                    ins = fn(h)
                    if is_dma:
                        ins.then_inc(sems[tok[0]], 16)
                    else:
                        if tok in sch.targets:
                            delta = tok[1] - cinc
                            ins.then_inc(sems["c_" + eng], delta)
                            cinc = tok[1]
                if eng in ("sync", "gpsimd"):
                    fin = {}
                    for tok in outtoks:
                        fin[tok[0]] = max(fin.get(tok[0], 0), tok[1])
                    for sem_name, val in sorted(fin.items()):
                        if known.get(sem_name, 0) < val:
                            h.wait_ge(sems[sem_name], val)
            return emit

        em = {e: make_emitter(e, per_eng[e]) for e in ENGS}

        @block.sync
        def _(h):
            em["sync"](h)

        @block.gpsimd
        def _(h):
            em["gpsimd"](h)

        @block.tensor
        def _(h):
            em["tensor"](h)

        @block.vector
        def _(h):
            em["vector"](h)

        @block.scalar
        def _(h):
            em["scalar"](h)

    lower_extended_insts(nc)
    return nc



def _host_inputs(inputs, c):
    b, hlf = c // 2, c % 2
    x = np.asarray(inputs["L0_attn_x"])
    own = x[b, hlf * TO:(hlf + 1) * TO, :]
    ctx = x[b, 0:TO, :] if hlf == 1 else np.zeros((TO, D), np.float32)
    x_fullv = np.concatenate([ctx, own], axis=0)

    pos = np.arange(4096, dtype=np.float32)
    dist = (L - 1.0) - pos
    if hlf == 1:
        posw = pos
        distw = dist
    else:
        posw = np.concatenate([np.full(TO, 1e9, np.float32), pos[0:TO]])
        distw = np.concatenate([np.full(TO, 1e9, np.float32), dist[0:TO]])
    ramp = np.empty((KH, T), np.float32)
    ramp[0:28, :] = distw[None, :]
    ramp[28:32, :] = posw[None, :]

    theta = np.asarray(inputs["L0_attn_theta"]).reshape(KH, 32, MT)
    th = np.zeros((128, KH), np.float32)
    for j in range(8):
        for m in range(MT):
            th[:, j * 4 + m] = theta[4 * j:4 * j + 4, :, m].reshape(128)

    slopes = np.concatenate([np.asarray(inputs["L0_attn_decay_slopes"]),
                             np.asarray(inputs["L0_attn_anchor_slopes"])])

    wcv = np.asarray(inputs["L0_attn_w_conv"]).reshape(4, TOTAL)
    wconv = np.zeros((128, 68), np.float32)
    bconv = np.zeros((128, 17), np.float32)
    bcv = np.asarray(inputs["L0_attn_b_conv"])
    for cc in range(17):
        wd = 128 if cc < 16 else 32
        wconv[0:wd, cc * 4:(cc + 1) * 4] = wcv[:, cc * 128:cc * 128 + wd].T
        bconv[0:wd, cc] = bcv[cc * 128:cc * 128 + wd]

    Wre = np.asarray(inputs["L0_attn_W_re"])
    Wim = np.asarray(inputs["L0_attn_W_im"])
    nsc = np.asarray(inputs["L0_attn_norm_scale"])
    perm = np.array([[h * 4 + m for h in range(32)] for m in range(4)]).reshape(128)
    Wre_m = Wre[perm]
    Wim_m = Wim[perm]
    nsr = nsc[0:128][perm].reshape(128, 1)
    nsi = nsc[128:256][perm].reshape(128, 1)

    em_ = np.zeros((KH, 8 * 128), np.float32)
    bo = np.zeros((128, 8 * KH), np.float32)
    for j in range(8):
        for col in range(128):
            em_[4 * j + col // 32, j * 128 + col] = 1.0
            bo[col, j * KH + 4 * j + col // 32] = 1.0

    cstv = np.tile(np.array([[PI / 2, TWO_PI, -TWO_PI, 0.0]], np.float32), (128, 1))

    return {
        "x_full": np.ascontiguousarray(x_fullv),
        "Win": np.asarray(inputs["L0_attn_W_in"]),
        "Wout": np.asarray(inputs["L0_attn_W_out"]),
        "theta_s": th,
        "score_col": np.asarray(inputs["L0_attn_score_scale"]).reshape(KH, 1),
        "slopes_col": slopes.reshape(KH, 1).astype(np.float32),
        "ramp32": ramp,
        "wconv_fm": wconv,
        "bconv_fm": bconv,
        "deriv": np.asarray(inputs["L0_attn_deriv_logits"]).reshape(1, 3),
        "Wre_m": np.ascontiguousarray(Wre_m),
        "Wim_m": np.ascontiguousarray(Wim_m),
        "ns_re": np.ascontiguousarray(nsr),
        "ns_im": np.ascontiguousarray(nsi),
        "router": np.asarray(inputs["L0_ffn_router_w"]),
        "w0": np.asarray(inputs["L0_ffn_w0_kernel"]),
        "w1": np.asarray(inputs["L0_ffn_w1_kernel"]),
        "wo": np.asarray(inputs["L0_ffn_wo_kernel"]),
        "ident": np.eye(128, dtype=np.float32),
        "ident16": np.eye(128, dtype=np.float16),
        "ones1": np.ones((1, 128), np.float32),
        "emat": em_.astype(ml_dtypes.bfloat16),
        "bones": bo.astype(ml_dtypes.bfloat16),
        "erow": np.tile(np.arange(E, dtype=np.float32)[None, :], (128, 1)),
        "cst": cstv,
    }


L = 4096


def kernel(**inputs):
    nc = build_program()
    in_maps = [_host_inputs(inputs, c) for c in range(8)]
    res = run_bass_kernel_spmd(nc, in_maps, core_ids=list(range(8)))
    out = np.empty((4, L, D), np.float32)
    for c in range(8):
        b, hlf = c // 2, c % 2
        out[b, hlf * TO:(hlf + 1) * TO, :] = np.asarray(res.results[c]["out"])
    return out


# version 08 (final)
# speedup vs baseline: 1.0114x; 1.0114x over previous
import numpy as np
import ml_dtypes
import concourse.bass as bass
import concourse.mybir as mybir
from concourse.bass_utils import run_bass_kernel_spmd
from concourse.library_config import mlp
from concourse.library_overlay import lower_extended_insts
from contextlib import ExitStack

F32 = mybir.dt.float32
F32R = mybir.dt.float32r
BF16 = mybir.dt.bfloat16
F16 = mybir.dt.float16
I32 = mybir.dt.int32
I16 = mybir.dt.int16
AF = mybir.ActivationFunctionType
ALU = mybir.AluOpType
AX = mybir.AxisListType

PI = float(np.pi)
TWO_PI = float(2 * np.pi)

D = 1024
TOTAL = 2080
KH = 32
MT = 4
T = 4096
TO = 2048
HH = 2048
E = 8
CAP = 512
HF = 5464
NSLOT = E * CAP + 1

ENGS = ("sync", "gpsimd", "tensor", "vector", "scalar")
NDPOOL = 6

DBG = []
STOP_PHASE = 99


class Sched:

    GAP = 12

    def __init__(self):
        self.ops = []
        self.ccount = {e: 0 for e in ENGS}
        self.dcount = {}
        self.dpool_i = {e: 0 for e in ENGS}
        self.ordc = {e: 0 for e in ENGS}
        self.last_w = {}
        self.readers = {}
        self.targets = set()

    def _collect(self, eng, reads, writes):
        waits = {}
        cur = self.ordc[eng]
        def need(tok, peng, pdma, pord):
            if peng == eng and not pdma:
                if peng == "tensor" or (cur - pord) >= self.GAP:
                    return
            sem, val = tok
            if waits.get(sem, 0) < val:
                waits[sem] = val
            self.targets.add(tok)
        for k in reads:
            if k in self.last_w:
                tok, we, wd, wo = self.last_w[k]
                need(tok, we, wd, wo)
        for k in writes:
            if k in self.last_w:
                tok, we, wd, wo = self.last_w[k]
                need(tok, we, wd, wo)
            for re_, (tok, rd, ro) in self.readers.get(k, {}).items():
                need(tok, re_, rd, ro)
        return waits

    def op(self, eng, fn, reads=(), writes=()):
        waits = self._collect(eng, reads, writes)
        self.ccount[eng] += 1
        self.ordc[eng] += 1
        tok = ("c_" + eng, self.ccount[eng])
        o = self.ordc[eng]
        self.ops.append((eng, fn, waits, tok, False))
        for k in reads:
            self.readers.setdefault(k, {})[eng] = (tok, False, o)
        for k in writes:
            self.last_w[k] = (tok, eng, False, o)
            self.readers[k] = {}
        return tok

    def dop(self, eng, fn, reads=(), writes=()):
        waits = self._collect(eng, reads, writes)
        sem = f"d_{eng}_{self.dpool_i[eng] % NDPOOL}"
        self.dpool_i[eng] += 1
        prev = self.dcount.get(sem, 0)
        if prev > 0 and waits.get(sem, 0) < prev:
            waits[sem] = prev
        self.dcount[sem] = prev + 16
        tok = (sem, prev + 16)
        self.ordc[eng] += 1
        o = self.ordc[eng]
        self.ops.append((eng, fn, waits, tok, True))
        for k in reads:
            self.readers.setdefault(k, {})[eng] = (tok, True, o)
        for k in writes:
            self.last_w[k] = (tok, eng, True, o)
            self.readers[k] = {}
        return tok


def build_program():
    nc = bass.Bass("TRN2", target_bir_lowering=False)

    def inp(name, shape, dt=F32):
        return nc.dram_tensor(name, list(shape), dt, kind="ExternalInput").ap()

    x_full = inp("x_full", [T, D])
    Win = inp("Win", [D, TOTAL])
    Wout = inp("Wout", [D, D])
    theta_s = inp("theta_s", [128, KH])
    score_col = inp("score_col", [KH, 1])
    slopes_col = inp("slopes_col", [KH, 1])
    ramp32 = inp("ramp32", [KH, T])
    wconv_fm = inp("wconv_fm", [128, 4 * 17])
    bconv_fm = inp("bconv_fm", [128, 17])
    deriv = inp("deriv", [1, 3])
    Wre_m = inp("Wre_m", [128, 32])
    Wim_m = inp("Wim_m", [128, 32])
    ns_re = inp("ns_re", [128, 1])
    ns_im = inp("ns_im", [128, 1])
    router = inp("router", [D, E])
    w0 = inp("w0", [E, D, HF])
    w1 = inp("w1", [E, D, HF])
    wo = inp("wo", [E, HF, D])
    ident = inp("ident", [128, 128])
    ident16 = inp("ident16", [128, 128], F16)
    ones1 = inp("ones1", [1, 128])
    emat = inp("emat", [KH, 8 * 128], BF16)
    bones = inp("bones", [128, 8 * KH], BF16)
    erow = inp("erow", [128, E])
    cst = inp("cst", [128, 4])

    out_hbm = nc.dram_tensor("out", [TO, D], F32, kind="ExternalOutput").ap()
    dbg_out = {}
    for name in DBG:
        dbg_out[name] = nc.dram_tensor("dbg_" + name, [128, 4096], F32,
                                       kind="ExternalOutput").ap()

    xbf_hbm = nc.dram_tensor("xbf_hbm", [T, D], BF16, kind="Internal").ap()
    ei_bf = nc.dram_tensor("ei_bf", [NSLOT, D], BF16, kind="Internal").ap()
    eo_hbm = nc.dram_tensor("eo_hbm", [NSLOT, D], BF16, kind="Internal").ap()
    idx_dram = nc.dram_tensor("idx_dram", [2, TO], F32, kind="Internal").ap()

    es = ExitStack()
    def sb(name, shape, dt=F32):
        return es.enter_context(nc.sbuf_tensor(name, list(shape), dt))

    c_theta = sb("c_theta", [128, KH]); c_thq = sb("c_thq", [128, KH])
    c_score = sb("c_score", [KH, 1]); c_slopes = sb("c_slopes", [KH, 1])
    lncol = sb("lncol", [128, 16])
    c_negsd = lncol[0:KH, 7:8]
    c_wconv = sb("c_wconv", [128, 68]); c_bconv = sb("c_bconv", [128, 17])
    c_derivt = sb("c_derivt", [128, 8]); c_deriv = c_derivt[0:1, 0:8]
    c_wcols = sb("c_wcols", [128, 4])
    c_wre = sb("c_wre", [128, 32]); c_wim = sb("c_wim", [128, 32])
    c_nsre = sb("c_nsre", [128, 1]); c_nsim = sb("c_nsim", [128, 1])
    c_ident = sb("c_ident", [128, 128]); c_ones = sb("c_ones", [1, 128])
    c_ident16 = sb("c_ident16", [128, 128], F16)
    c_emat = sb("c_emat", [KH, 8 * 128], BF16)
    c_bones = sb("c_bones", [128, 8 * KH], BF16)
    c_erow = sb("c_erow", [128, E]); c_cst = sb("c_cst", [128, 4])
    c_router = sb("c_router", [128, 8 * E], F32R)
    blk = sb("blk", [128, 8 * 128], BF16)
    wre_f = sb("wre_f", [128, 64]); wre_b = sb("wre_b", [128, 64], BF16)

    AR1 = sb("AR1", [128, 32768], BF16)
    AR2 = sb("AR2", [128, 34816], BF16)
    AB_ = sb("AB", [128, 33024], BF16)

    xT = AR1
    x1 = AR1[:, 0:32768].bitcast(F32)

    ugate = AR2[:, 0:16384]
    zer = AR2[:, 0:2048]
    zc = AR2[:, 16384:20480]
    s1b = AR2[:, 20480:22528]
    scale_bf_v = s1b[0:KH, :]
    s2b = AR2[:, 22528:24576]
    r1b = AR2[:, 24576:26624]
    r2b = AR2[:, 26624:28672]
    cs1 = r1b
    cs2 = r2b
    ppw = AR2[:, 28672:30720]
    pwe = AR2[:, 30720:34816]
    sexp = AR2[:, 30720:32768]
    wout_bf = AR2[:, 16384:24576]
    wob = AR2[:, 0:16384]
    hidb = AR2[:, 16384:24576]
    gelb = AR2[:, 24576:25088]
    gth = AR2[:, 0:32768]
    x1bf = AR2[:, 16384:32768]

    zbuf = AB_[:, 0:4102]
    arn1 = AB_[:, 4352:12544].bitcast(F32)
    arn2 = AB_[:, 12544:16640].bitcast(F32)
    invd_o = AB_[:, 16640:20736].bitcast(F32)[0:KH, :]
    pwb = AB_[:, 20736:24832][0:KH, :]
    mscol = arn1[0:KH, 0:TO]
    iv2c = arn1[0:KH, TO:T]
    scl32 = arn2[0:KH, 0:TO]
    sraw_v = AR2[:, 2048:10240].bitcast(F32)[0:KH, :]
    ramp_v = AR2[:, 10240:18432].bitcast(F32)[0:KH, :]
    den_v = AR2[:, 18432:26624].bitcast(F32)[0:KH, :]
    scrA = AR2[:, 26624:34816].bitcast(F32)[0:KH, :]
    xtile = AB_[:, 0:4096].bitcast(F32)
    lnc = AB_[:, 4352:6400].bitcast(F32)
    lnc2 = AB_[:, 0:2048].bitcast(F32)
    m1T = AB_[:, 4352:6400].bitcast(F16)[0:E, :]
    m2T = AB_[:, 6400:8448].bitcast(F16)[0:E, :]
    p1T = AB_[:, 8448:10496].bitcast(F16)[0:E, :]
    p2T = AB_[:, 10496:12544].bitcast(F16)[0:E, :]
    k1T = AB_[:, 12544:14592].bitcast(F16)[0:E, :]
    k2T = AB_[:, 14592:16640].bitcast(F16)[0:E, :]
    eosb = AB_[:, 0:8192].bitcast(F32)
    eobf = AB_[:, 8192:12288]
    eib = AB_[:, 12288:16384]
    w0b = AB_[:, 16384:24576]
    w1b = AR2[:, 25088:33280]
    gelb2 = AR2[:, 33280:33792]

    carry = sb("carry", [128, 8])
    eib2 = AB_[:, 26880:30976]
    sqb = AB_[:, 30976:33024]
    wbuf = AB_[:, 24832:26880]
    vcols = sb("vcols", [128, 16])
    cntT = vcols[0:E, 14:16]
    glog = sb("glog", [128, 16 * E])
    x1t = sb("x1t", [128, 128], F32R)
    g8a = sb("g8a", [128, E]); g8b = sb("g8b", [128, E]); g8c = sb("g8c", [128, E])
    m1tm = sb("m1tm", [128, 16 * E]); m2tm = sb("m2tm", [128, 16 * E])
    gcols = sb("gcols", [128, 16 * 8])
    slotff = sb("slotff", [128, 32], F32)
    sloti = sb("sloti", [128, 2 * 128], I16)
    stb32 = AB_[:, 0:256].bitcast(F32)[0:32, :]
    slotbt = AB_[:, 256:320].bitcast(F32)
    bi16c = AB_[:, 320:448].bitcast(I16)[0:32, :]

    ps = [es.enter_context(nc.psum_tensor(f"ps{i}", [128, 512], F32)) for i in range(8)]
    PK = [f"ps{i}" for i in range(8)]
    ps2h = ps[2][:, 0:4].bitcast(F16)

    sems = {}
    for e in ENGS:
        sems["c_" + e] = es.enter_context(nc.semaphore("c_" + e))
        for i in range(NDPOOL):
            sems[f"d_{e}_{i}"] = es.enter_context(nc.semaphore(f"d_{e}_{i}"))

    sch = Sched()
    O, DO = sch.op, sch.dop

    def dma(eng, out, in_, reads, writes, **kw):
        return DO(eng, lambda h: h.dma_start(out=out, in_=in_, **kw), reads, writes)

    def mm(lhsT, rhs, out, start, stop, reads, writes):
        return O("tensor",
                 lambda h: h.matmul(out, lhsT, rhs, start=start, stop=stop),
                 reads, writes)

    def trans(out, in_, reads, writes):
        p_ = in_.shape[0]
        idt = c_ident16 if in_.dtype == F16 else c_ident
        return O("tensor", lambda h: h.transpose(out, in_, idt[0:p_, 0:p_]),
                 tuple(reads) + ("c_ident",), writes)

    def v(fn, reads, writes):
        return O("vector", fn, reads, writes)

    def s(fn, reads, writes):
        return O("scalar", fn, reads, writes)

    def dbg(name, ap, reads):
        if name in DBG:
            w_ = min(ap.shape[-1], 4096)
            p_ = ap.shape[0]
            def fn(h):
                return h.dma_start(out=dbg_out[name][0:p_, 0:w_], in_=ap[:, 0:w_],
                                   allow_slow_non_contiguous=True)
            DO("gpsimd", fn, tuple(reads), tuple(reads))

    outtoks_g = []

    O("gpsimd", lambda h: h.load_library(mlp), (), ())

    for o, i, k in [
        (c_theta[:], theta_s, "c_theta"), (c_score[:], score_col, "c_score"),
        (c_slopes[:], slopes_col, "c_slopes"), (c_wconv[:], wconv_fm, "c_wconv"),
        (c_bconv[:], bconv_fm, "c_bconv"), (c_deriv[0:1, 0:3], deriv, "c_deriv"),
        (c_wre[:], Wre_m, "c_wre"), (c_wim[:], Wim_m, "c_wim"),
        (c_nsre[:], ns_re, "c_nsre"), (c_nsim[:], ns_im, "c_nsim"),
        (c_ident[:], ident, "c_ident"), (c_ones[:], ones1, "c_ones"),
        (c_ident16[:], ident16, "c_ident"),
        (c_emat[:], emat, "c_emat"), (c_bones[:], bones, "c_bones"),
        (c_erow[:], erow, "c_erow"), (c_cst[:], cst, "c_cst"),
        (ramp_v, ramp32, "ramp"),
    ]:
        dma("sync", o, i, (), (k,))
    dma("sync", c_router[:].rearrange("p (c e) -> p c e", e=E),
        router.rearrange("(c p) e -> p c e", p=128).bitcast(F32R), (), ("c_router",))

    nc.const_aps.register(F32, PI / 2, c_cst[:, 0:1])
    nc.const_aps.register(F32, TWO_PI, c_cst[:, 1:2])
    nc.const_aps.register(F32, -TWO_PI, c_cst[:, 2:3])

    v(lambda h: h.tensor_scalar_mul(c_thq[:], c_theta[:], 1.0 / TWO_PI),
      ("c_theta",), ("c_thq",))
    s(lambda h: h.activation(lncol[0:KH, 0:1], c_slopes[:], AF.Abs),
      ("c_slopes",), ("sp0",))
    v(lambda h: h.tensor_scalar_mul(lncol[0:KH, 1:2], lncol[0:KH, 0:1], -1.0),
      ("sp0",), ("sp1",))
    s(lambda h: h.activation(lncol[0:KH, 2:3], lncol[0:KH, 1:2], AF.Exp),
      ("sp1",), ("sp2",))
    v(lambda h: h.tensor_scalar(lncol[0:KH, 3:4], lncol[0:KH, 2:3], 1.0, None,
                                ALU.add), ("sp2",), ("sp3",))
    s(lambda h: h.activation(lncol[0:KH, 4:5], lncol[0:KH, 3:4], AF.Ln),
      ("sp3",), ("sp4",))
    v(lambda h: h.tensor_scalar(lncol[0:KH, 5:6], c_slopes[:], 0.0, None,
                                ALU.max), ("c_slopes",), ("sp5",))
    v(lambda h: h.tensor_scalar(lncol[0:KH, 6:7], lncol[0:KH, 4:5],
                                lncol[0:KH, 5:6], None, ALU.add),
      ("sp4", "sp5"), ("sp6",))
    v(lambda h: h.tensor_scalar_mul(c_negsd, lncol[0:KH, 6:7], -1.0),
      ("sp6",), ("c_negsd",))
    v(lambda h: h.reduce_max(c_deriv[0:1, 3:4], c_deriv[0:1, 0:3], axis=AX.X,
                             negate=True), ("c_deriv",), ("dmx",))
    s(lambda h: h.activation(c_deriv[0:1, 0:3], c_deriv[0:1, 0:3], AF.Exp,
                             bias=c_deriv[0:1, 3:4], scale=1.0),
      ("c_deriv", "dmx"), ("dex",))
    v(lambda h: h.reduce_sum(c_deriv[0:1, 3:4], c_deriv[0:1, 0:3], axis=AX.X),
      ("dex",), ("dsum",))
    v(lambda h: h.reciprocal(c_deriv[0:1, 3:4], c_deriv[0:1, 3:4]),
      ("dsum",), ("drec",))
    v(lambda h: h.tensor_scalar_mul(c_deriv[0:1, 0:3], c_deriv[0:1, 0:3],
                                    c_deriv[0:1, 3:4]), ("drec", "dex"), ("dw",))
    mm(c_ones[:], c_deriv[0:1, 0:3], ps[2][:, 0:3], True, True,
       ("dw", "c_ones"), ("ps2",))
    v(lambda h: h.tensor_copy(c_wcols[:, 0:3], ps[2][:, 0:3]), ("ps2",), ("wc",))
    v(lambda h: h.tensor_scalar_mul(c_wcols[:, 3:4], c_wcols[:, 2:3], -1.0),
      ("wc",), ("wcn",))
    v(lambda h: h.tensor_scalar_mul(wre_f[:, 0:32], c_wre[:], c_nsre[:, 0:1]),
      ("c_wre", "c_nsre"), ("wrf",))
    v(lambda h: h.tensor_scalar_mul(wre_f[:, 32:64], c_wim[:], c_nsim[:, 0:1]),
      ("c_wim", "c_nsim"), ("wrf2",))
    v(lambda h: h.tensor_copy(wre_b[:], wre_f[:]), ("wrf", "wrf2"), ("wrb",))
    v(lambda h: h.memset(blk[:], 0.0), (), ("blk",))
    for m in range(MT):
        for ri in range(2):
            for r in range(4):
                def fn(h, m=m, ri=ri, r=r):
                    return h.dma_start(
                        out=blk[32 * r:32 * r + 32,
                                (m * 2 + ri) * 128 + 32 * r:
                                (m * 2 + ri) * 128 + 32 * r + 32],
                        in_=wre_b[m * 32:(m + 1) * 32, ri * 32:ri * 32 + 32])
                DO("sync", fn, ("wrb",), ("blk",))


    for dc in range(8):
        def fn(h, dc=dc):
            return h.dma_start(out=xbf_hbm[:, dc * 128:(dc + 1) * 128],
                               in_=x_full[:, dc * 128:(dc + 1) * 128],
                               max_dma_last_dim=4096)
        DO("gpsimd", fn, (), (f"xbf{dc}",))
        def fn(h, dc=dc):
            return h.dma_start(out=xT[:, dc * T:(dc + 1) * T],
                               in_=xbf_hbm[:, dc * 128:(dc + 1) * 128],
                               transpose=True)
        DO("sync", fn, (f"xbf{dc}",), (f"xT{dc}",))
    XTK = tuple(f"xT{dc}" for dc in range(8))

    v(lambda h: h.memset(zer, 0.0), (), ("zer",))
    zerf = zer.bitcast(F32)
    for i in range(32):
        def fn(h, i=i):
            return h.dma_start(out=ei_bf[i * 128:(i + 1) * 128, :],
                               in_=zer[:, 0:D])
        DO("sync", fn, ("zer",), ("ei_z",))
    dma("sync", eo_hbm[E * CAP:NSLOT, :], zer[0:1, 0:D], ("zer",), ("eo_z",))

    def win_load(n, cc):
        w = 128 if cc < 16 else 32
        slot = n % 2
        def fn(h):
            return h.dma_start(
                out=wbuf[:, slot * 1024:slot * 1024 + 8 * w].rearrange(
                    "p (dc c) -> p dc c", dc=8),
                in_=Win[:, cc * 128:cc * 128 + w].rearrange("(dc p) c -> p dc c", p=128),
                max_dma_last_dim=4096)
        DO("gpsimd", fn, (), (f"wbuf{slot}",))

    def zmm(n, cc, nrow):
        w = 128 if cc < 16 else 32
        slot = n % 2
        for ts_ in range(8):
            bank = ts_ % 2
            for dc in range(8):
                mm(wbuf[:, slot * 1024 + dc * w: slot * 1024 + (dc + 1) * w],
                   xT[:, dc * T + ts_ * 512: dc * T + (ts_ + 1) * 512],
                   ps[bank][0:w, 0:512], dc == 0, dc == 7,
                   (f"wbuf{slot}",) + XTK, (PK[bank],))
            def fn(h, ts_=ts_, bank=bank, w=w):
                return h.tensor_copy(zbuf[0:w, 3 + ts_ * 512: 3 + (ts_ + 1) * 512],
                                     ps[bank][0:w, 0:512])
            v(fn, (PK[bank],), ("zbuf",))

    def conv_emit(w, ccol, out_ap, out_key):
        v(lambda h: h.tensor_scalar(arn1[0:w, :], zbuf[0:w, 0:T],
                                    c_wconv[0:w, ccol * 4:ccol * 4 + 1],
                                    c_bconv[0:w, ccol:ccol + 1], ALU.mult, ALU.add),
          ("zbuf", "c_wconv", "c_bconv"), ("arn1",))
        for tap in (1, 2):
            def fn(h, tap=tap):
                return h.scalar_tensor_tensor(
                    arn1[0:w, :], zbuf[0:w, tap:tap + T],
                    c_wconv[0:w, ccol * 4 + tap:ccol * 4 + tap + 1], arn1[0:w, :],
                    ALU.mult, ALU.add)
            v(fn, ("zbuf", "arn1"), ("arn1",))
        def fn(h):
            return h.scalar_tensor_tensor(
                out_ap, zbuf[0:w, 3:3 + T],
                c_wconv[0:w, ccol * 4 + 3:ccol * 4 + 4], arn1[0:w, :],
                ALU.mult, ALU.add)
        v(fn, ("zbuf", "arn1"), (out_key,))

    if STOP_PHASE < 1:
        return _finish(nc, es, sch, sems, outtoks_g, ENGS)
    v(lambda h: h.memset(zbuf[:, 0:3], 0.0), (), ("zbuf",))
    win_load(0, 16)
    zmm(0, 16, 32)
    conv_emit(32, 16, sraw_v, "sraw")
    dbg("sraw0", sraw_v, ("sraw",))
    v(lambda h: h.tensor_scalar(sraw_v, sraw_v, c_score[0:KH, 0:1], 20.0,
                                ALU.mult, ALU.min), ("sraw", "c_score"), ("sraw",))
    v(lambda h: h.tensor_scalar(sraw_v, sraw_v, -20.0, None, ALU.max),
      ("sraw",), ("sraw",))
    v(lambda h: h.scalar_tensor_tensor(sraw_v, ramp_v, c_negsd,
                                       sraw_v, ALU.mult, ALU.add),
      ("ramp", "c_negsd", "sraw"), ("sraw",))
    dbg("negsd", c_negsd, ("c_negsd",))
    dbg("slopes", c_slopes, ("c_slopes",))
    dbg("lncol", lncol, ("sp6", "sp4", "sp5", "sp2"))
    dbg("ramp", ramp_v, ("ramp",))
    dbg("arg", sraw_v, ("sraw",))
    s(lambda h: h.activation(pwb, sraw_v, AF.Exp), ("sraw",), ("pw",))
    dbg("pw", pwb, ("pw",))
    v(lambda h: h.tensor_tensor_scan(den_v, pwb, pwb, 0.0, ALU.add, ALU.bypass),
      ("pw",), ("den",))
    v(lambda h: h.tensor_scalar(scrA[:, 0:TO], den_v[:, HH:T], 1e-4, None,
                                ALU.max), ("den",), ("scrA",))
    v(lambda h: h.reciprocal(invd_o, scrA[:, 0:TO]), ("scrA",), ("invd",))

    if STOP_PHASE < 2:
        return _finish(nc, es, sch, sems, outtoks_g, ENGS)
    for n in range(1, 9):
        cc = n + 7
        j = cc - 8
        win_load(n, cc)
        zmm(n, cc, 128)
        conv_emit(128, cc, zc[:, 0:T], "zc")
        def fn(h, j=j):
            return h.activation(ugate[:, j * TO:(j + 1) * TO], zc[:, HH:T], AF.Silu)
        s(fn, ("zc",), (f"gate{j}",) + (("zer",) if j == 0 else ()))

    if STOP_PHASE < 3:
        return _finish(nc, es, sch, sems, outtoks_g, ENGS)
    A = arn1[:, 0:HH]
    Bv = arn1[:, HH:T]
    for n in range(9, 17):
        j = n - 9
        win_load(n, j)
        zmm(n, j, 128)
        conv_emit(128, j, zc[:, 0:T], "zc")
        if j == 0:
            dbg("zc0", zc[:, 0:2048].bitcast(BF16), ("zc",))
        for ts_ in range(8):
            mm(c_emat[:, j * 128:(j + 1) * 128],
               pwb[:, ts_ * 512:(ts_ + 1) * 512],
               ps[2][:, 0:512], True, True, ("c_emat", "pw"), ("ps2",))
            def fn(h, ts_=ts_):
                return h.activation(pwe[:, ts_ * 512:(ts_ + 1) * 512],
                                    ps[2][:, 0:512], AF.Copy)
            s(fn, ("ps2",), ("pwe",))
        for hh in range(2):
            sl = slice(hh * HH, (hh + 1) * HH)
            def fn(h, sl=sl):
                return h.tensor_scalar(A, zc[:, sl], c_wcols[:, 3:4],
                                       c_wcols[:, 1:2], ALU.mult, ALU.add)
            v(fn, ("zc", "wcn", "wc"), ("arn1",))
            def fn(h, sl=sl):
                return h.tensor_tensor(arn2[:, :], A, zc[:, sl], ALU.mult)
            v(fn, ("arn1", "zc"), ("arn2",))
            def fn(h, sl=sl):
                return h.scalar_tensor_tensor(ppw[:, :], arn2[:, :], c_wcols[:, 0:1],
                                              pwe[:, sl], ALU.add, ALU.mult)
            v(fn, ("arn2", "wc", "pwe"), ("ppw",))
            if hh == 1:
                tails = []
            for m in range(MT):
                tc = j * 4 + m
                def fn(h, sl=sl, tc=tc):
                    return h.activation(A, zc[:, sl], AF.Copy,
                                        bias=0.0, scale=c_thq[:, tc:tc + 1])
                s(fn, ("zc", "c_thq"), ("arn1",))
                s(lambda h: h.activation(Bv.bitcast(I32), A, AF.Copy),
                  ("arn1",), ("arn1",))
                if hh == 0:
                    v(lambda h: h.tensor_copy(arn2[:, :], Bv.bitcast(I32)),
                      ("arn1",), ("arn2",))
                else:
                    s(lambda h: h.activation(arn2[:, :], Bv.bitcast(I32), AF.Copy),
                      ("arn1",), ("arn2",))
                v(lambda h: h.scalar_tensor_tensor(Bv, arn2[:, :], -1.0, A,
                                                   ALU.mult, ALU.add),
                  ("arn2", "arn1"), ("arn1",))
                if hh == 1:
                    for t_ in tails:
                        t_()
                    tails = []
                s(lambda h: h.activation(s1b[:, :], Bv, AF.Sin, bias=0.0,
                                         scale=TWO_PI), ("arn1",), ("s1b",))
                s(lambda h: h.activation(arn2[:, :], Bv, AF.Abs), ("arn1",), ("arn2",))
                s(lambda h: h.activation(s2b[:, :], arn2[:, :], AF.Sin,
                                         bias=PI / 2, scale=-TWO_PI),
                  ("arn2",), ("s2b",))
                cm = (f"cr{m}", f"ci{m}")
                if hh == 0:
                    def fn(h, m=m):
                        return h.scalar_tensor_tensor(
                            r1b[:, :], ppw[:, :], 1.0, s2b[:, :],
                            ALU.mult, ALU.mult,
                            accum_out=carry[:, 2 * m:2 * m + 1])
                    v(fn, ("ppw", "s2b"), (cm[0], "r1b"))
                    if j == 0 and m == 0:
                        dbg("mgc", r1b, ("r1b",))
                        dbg("uu", A, ("arn1",))
                        dbg("rr", Bv, ("arn1",))
                        dbg("sinv", s1b, ("s1b",))
                        dbg("cosv", s2b, ("s2b",))
                        dbg("ppwv", ppw, ("ppw",))
                        dbg("wcols", c_wcols, ("wc", "wcn"))
                        dbg("pwev", pwe, ("pwe",))
                        dbg("pwbv", pwb, ("pw",))
                        dbg("carry0", carry, (cm[0],))
                    def fn(h, m=m):
                        return h.scalar_tensor_tensor(
                            r2b[:, :], ppw[:, :], 1.0, s1b[:, :],
                            ALU.mult, ALU.mult,
                            accum_out=carry[:, 2 * m + 1:2 * m + 2])
                    v(fn, ("ppw", "s1b"), (cm[1], "r2b"))
                else:
                    v(lambda h: h.tensor_tensor(r1b[:, :], ppw[:, :], s2b[:, :],
                                                ALU.mult), ("ppw", "s2b"), ("r1b",))
                    v(lambda h: h.tensor_tensor(r2b[:, :], ppw[:, :], s1b[:, :],
                                                ALU.mult), ("ppw", "s1b"), ("r2b",))
                    def fn(h, m=m):
                        return h.tensor_tensor_scan(cs1[:, :], r1b[:, :], r1b[:, :],
                                                    carry[:, 2 * m:2 * m + 1],
                                                    ALU.add, ALU.bypass)
                    v(fn, ("r1b", cm[0]), ("r1b",))
                    def fn(h, m=m):
                        return h.tensor_tensor_scan(cs2[:, :], r2b[:, :], r2b[:, :],
                                                    carry[:, 2 * m + 1:2 * m + 2],
                                                    ALU.add, ALU.bypass)
                    v(fn, ("r2b", cm[1]), ("r2b",))
                    if j == 0 and m == 0:
                        dbg("cs00", cs1, ("r1b",))
                        dbg("ppw0", ppw, ("ppw",))
                        dbg("s2b0", s2b, ("s2b",))
                        dbg("u0arg", A, ("arn1",))
                        dbg("pwe0", pwe, ("pwe",))
                        dbg("poly0", arn2, ("arn2",))
                    for ri in range(2):
                        csb, ck = (cs1, "r1b") if ri == 0 else (cs2, "r2b")
                        first = (m == 0 and ri == 0)
                        last = (m == 3 and ri == 1)
                        bcol = (m * 2 + ri) * 128
                        for ts_ in range(4):
                            mm(blk[:, bcol:bcol + 128],
                               csb[:, ts_ * 512:(ts_ + 1) * 512],
                               ps[ts_][:, 0:512], first, last,
                               ("blk", ck), (PK[ts_],))
                        def tail(csb=csb, ck=ck, first=first, last=last, j=j):
                            def fn(h, csb=csb):
                                return h.tensor_tensor(sqb[:, :], csb[:], csb[:],
                                                       ALU.mult)
                            v(fn, (ck,), ("sqb",))
                            for ts_ in range(4):
                                mm(c_bones[:, j * KH:(j + 1) * KH],
                                   sqb[:, ts_ * 512:(ts_ + 1) * 512],
                                   ps[4 + ts_][0:KH, 0:512], first, last,
                                   ("c_bones", "sqb"), (PK[4 + ts_],))
                        tails.append(tail)
            if hh == 1:
                for t_ in tails:
                    t_()
                tails = []
        for ts_ in range(4):
            def fn(h, ts_=ts_):
                return h.activation(mscol[:, ts_ * 512:(ts_ + 1) * 512],
                                    ps[4 + ts_][0:KH, 0:512], AF.Copy)
            s(fn, (PK[4 + ts_],), ("arn1",))
        v(lambda h: h.tensor_tensor(iv2c, invd_o, invd_o, ALU.mult),
          ("invd",), ("arn1",))
        v(lambda h: h.tensor_tensor(scl32, mscol, iv2c, ALU.mult),
          ("arn1",), ("arn2",))
        v(lambda h: h.tensor_scalar(scl32, scl32, 1.0 / 256.0, 1e-5,
                                    ALU.mult, ALU.add), ("arn2",), ("arn2",))
        s(lambda h: h.activation(iv2c, scl32, AF.Sqrt), ("arn2",), ("arn1",))
        v(lambda h: h.reciprocal(mscol, iv2c), ("arn1",), ("arn1",))
        v(lambda h: h.tensor_tensor(scl32, invd_o, mscol, ALU.mult),
          ("invd", "arn1"), ("arn2",))
        v(lambda h: h.tensor_copy(scale_bf_v, scl32), ("arn2",), ("s1b",))
        for ts_ in range(4):
            mm(c_emat[:, j * 128:(j + 1) * 128],
               scale_bf_v[:, ts_ * 512:(ts_ + 1) * 512],
               ps[4 + ts_][:, 0:512], True, True,
               ("c_emat", "s1b"), (PK[4 + ts_],))
            def fn(h, ts_=ts_):
                return h.activation(sexp[:, ts_ * 512:(ts_ + 1) * 512],
                                    ps[4 + ts_][:, 0:512], AF.Copy)
            s(fn, (PK[4 + ts_],), ("pwe",))
        for ts_ in range(4):
            def fn(h, ts_=ts_):
                return h.tensor_tensor(s1b[:, ts_ * 512:(ts_ + 1) * 512],
                                       ps[ts_][:, 0:512],
                                       sexp[:, ts_ * 512:(ts_ + 1) * 512], ALU.mult)
            v(fn, (PK[ts_], "sexp"), ("s1b",))
        def fn(h, j=j):
            return h.tensor_tensor(ugate[:, j * TO:(j + 1) * TO], s1b[:, :],
                                   ugate[:, j * TO:(j + 1) * TO], ALU.mult)
        v(fn, ("s1b", f"gate{j}"), (f"u{j}",))
        if j == 0:
            dbg("u0", ugate[:, 0:TO], ("u0",))

    if STOP_PHASE < 5:
        return _finish(nc, es, sch, sems, outtoks_g, ENGS)
    def fn(h):
        return h.dma_start(out=wout_bf[:, :].rearrange("p (c d) -> p c d", c=8),
                           in_=Wout.rearrange("(c p) d -> p c d", p=128),
                           max_dma_last_dim=4096)
    DO("gpsimd", fn, ("zc", "s1b", "s2b", "r1b"),
       ("woutb", "zc", "s1b", "s2b", "r1b"))
    ureads = tuple(f"u{j}" for j in range(8))
    for i in range(16):
        def fn(h, i=i):
            return h.dma_start(
                out=xtile[:, (i % 2) * D:(i % 2 + 1) * D],
                in_=x_full.rearrange("(g q) m -> q g m", q=128)[:, 16 + i, :])
        DO("sync", fn, (), (f"xt{i % 2}",) + (("zbuf",) if i < 2 else ()))
        for half in range(2):
            for cc2 in range(8):
                mm(ugate[:, cc2 * TO + i * 128:cc2 * TO + (i + 1) * 128],
                   wout_bf[:, cc2 * D + half * 512:cc2 * D + (half + 1) * 512],
                   ps[half][:, 0:512], cc2 == 0, cc2 == 7,
                   ureads + ("woutb",), (PK[half],))
        for half in range(2):
            def fn(h, i=i, half=half):
                return h.tensor_tensor(
                    lnc[:, half * 512:(half + 1) * 512], ps[half][:, 0:512],
                    xtile[:, (i % 2) * D + half * 512:(i % 2) * D + (half + 1) * 512],
                    ALU.add)
            v(fn, (PK[half], f"xt{i % 2}"), ("lnc",))
        if i == 0:
            dbg("xr0", lnc, ("lnc",))
        v(lambda h: h.reduce_sum(lncol[:, 0:1], lnc[:, :], axis=AX.X),
          ("lnc",), ("ln0",))
        v(lambda h: h.tensor_scalar_mul(lncol[:, 1:2], lncol[:, 0:1], 1.0 / D),
          ("ln0",), ("ln1",))
        def fn(h, i=i):
            return h.tensor_scalar(x1[:, i * D:(i + 1) * D], lnc[:, :],
                                   lncol[:, 1:2], None, ALU.subtract)
        v(fn, ("lnc", "ln1"), (f"x1c{i}",) + (("zer",) if i < 4 else ()))
        def fn(h, i=i):
            return h.activation(lnc[:, :], x1[:, i * D:(i + 1) * D], AF.Square)
        s(fn, (f"x1c{i}",), ("lnc",))
        def fn(h, i=i):
            return h.reduce_sum(vcols[:, i:i + 1], lnc[:, :], axis=AX.X)
        v(fn, ("lnc",), (f"vc{i}",))
    vreads = tuple(f"vc{i}" for i in range(16))
    v(lambda h: h.tensor_scalar(vcols[:], vcols[:], 1.0 / D, 1e-5,
                                ALU.mult, ALU.add), vreads, ("vca",))
    s(lambda h: h.activation(vcols[:], vcols[:], AF.Sqrt), ("vca",), ("vcs",))
    v(lambda h: h.reciprocal(vcols[:], vcols[:]), ("vcs",), ("vcr",))
    for i in range(16):
        def fn(h, i=i):
            return h.tensor_scalar_mul(x1[:, i * D:(i + 1) * D],
                                       x1[:, i * D:(i + 1) * D], vcols[:, i:i + 1])
        v(fn, ("vcr", f"x1c{i}"), (f"x1_{i}",))
    dbg("x1", x1[:, 0:4096], tuple(f"x1_{i}" for i in range(4)))

    if STOP_PHASE < 6:
        return _finish(nc, es, sch, sems, outtoks_g, ENGS)
    for i in range(16):
        for mc in range(8):
            trans(ps[2][:, 0:128], x1[:, i * D + mc * 128:i * D + (mc + 1) * 128],
                  (f"x1_{i}",), ("ps2",))
            v(lambda h: h.tensor_copy(x1t[:], ps[2][:, 0:128]), ("ps2",), ("x1t",))
            mm(x1t[:], c_router[:, mc * E:(mc + 1) * E],
               ps[3][:, 0:E], mc == 0, mc == 7, ("x1t", "c_router"), ("ps3",))
        def fn(h, i=i):
            return h.tensor_copy(glog[:, i * E:(i + 1) * E], ps[3][:, 0:E])
        v(fn, ("ps3",), (f"lg{i}",))
    dbg("glog", glog[:], tuple(f"lg{i}" for i in range(16)))

    if STOP_PHASE < 7:
        return _finish(nc, es, sch, sems, outtoks_g, ENGS)
    for i in range(16):
        lg = glog[:, i * E:(i + 1) * E]
        raw = m2tm[:, i * E:(i + 1) * E]
        def fn(h, lg=lg):
            return h.reduce_max(lncol[:, 2:3], lg, axis=AX.X, negate=True)
        v(fn, (f"lg{i}",), ("gnm",))
        def fn(h, lg=lg):
            return h.activation(g8a[:], lg, AF.Exp, bias=lncol[:, 2:3], scale=1.0)
        s(fn, (f"lg{i}", "gnm"), ("g8a",))
        v(lambda h: h.reduce_sum(lncol[:, 3:4], g8a[:], axis=AX.X),
          ("g8a",), ("gss",))
        v(lambda h: h.reciprocal(lncol[:, 3:4], lncol[:, 3:4]), ("gss",), ("gsr",))
        def fn(h, raw=raw):
            return h.tensor_scalar_mul(raw, g8a[:], lncol[:, 3:4])
        v(fn, ("g8a", "gsr"), ("raw",))
        def fn(h, raw=raw, i=i):
            return h.reduce_max(gcols[:, i * 8 + 0:i * 8 + 1], raw, axis=AX.X)
        v(fn, ("raw",), ("mx1",))
        def fn(h, raw=raw, i=i):
            return h.tensor_scalar(g8a[:], raw, gcols[:, i * 8 + 0:i * 8 + 1],
                                   None, ALU.is_ge)
        v(fn, ("raw", "mx1"), ("eq1",))
        v(lambda h: h.tensor_tensor_scan(g8b[:], g8a[:], g8a[:], 0.0, ALU.add,
                                         ALU.bypass), ("eq1",), ("cum1",))
        v(lambda h: h.tensor_scalar(g8b[:], g8b[:], 1.5, None, ALU.is_le),
          ("cum1",), ("cl1",))
        v(lambda h: h.tensor_tensor(g8a[:], g8a[:], g8b[:], ALU.mult),
          ("eq1", "cl1"), ("m1v",))
        def fn(h, raw=raw):
            return h.scalar_tensor_tensor(g8c[:], g8a[:], -1e30, raw,
                                          ALU.mult, ALU.add)
        v(fn, ("m1v", "raw"), ("raw2",))
        def fn(h, i=i):
            return h.reduce_max(gcols[:, i * 8 + 1:i * 8 + 2], g8c[:], axis=AX.X)
        v(fn, ("raw2",), ("mx2",))
        def fn(h, i=i):
            return h.tensor_scalar(g8c[:], g8c[:], gcols[:, i * 8 + 1:i * 8 + 2],
                                   None, ALU.is_ge)
        v(fn, ("raw2", "mx2"), ("eq2",))
        v(lambda h: h.tensor_tensor_scan(g8b[:], g8c[:], g8c[:], 0.0, ALU.add,
                                         ALU.bypass), ("eq2",), ("cum2",))
        v(lambda h: h.tensor_scalar(g8b[:], g8b[:], 1.5, None, ALU.is_le),
          ("cum2",), ("cl2",))
        v(lambda h: h.tensor_tensor(g8c[:], g8c[:], g8b[:], ALU.mult),
          ("eq2", "cl2"), ("m2v",))
        v(lambda h: h.tensor_tensor(g8b[:], g8a[:], c_erow[:], ALU.mult),
          ("m1v", "c_erow"), ("i1t",))
        def fn(h, i=i):
            return h.reduce_sum(gcols[:, i * 8 + 2:i * 8 + 3], g8b[:], axis=AX.X)
        v(fn, ("i1t",), (f"i1_{i}",))
        v(lambda h: h.tensor_tensor(g8b[:], g8c[:], c_erow[:], ALU.mult),
          ("m2v", "c_erow"), ("i2t",))
        def fn(h, i=i):
            return h.reduce_sum(gcols[:, i * 8 + 3:i * 8 + 4], g8b[:], axis=AX.X)
        v(fn, ("i2t",), (f"i2_{i}",))
        def fn(h, i=i):
            return h.tensor_copy(m1tm[:, i * E:(i + 1) * E], g8a[:])
        v(fn, ("m1v",), (f"m1_{i}",))
        def fn(h, i=i):
            return h.tensor_copy(m2tm[:, i * E:(i + 1) * E], g8c[:])
        v(fn, ("m2v",), (f"m2_{i}",))
        trans(ps[2][0:E, 0:128], m1tm[:, i * E:(i + 1) * E], (f"m1_{i}",), ("ps2",))
        def fn(h, i=i):
            return h.tensor_copy(m1T[:, i * 128:(i + 1) * 128], ps[2][0:E, 0:128])
        v(fn, ("ps2",), ("m1T",))
        trans(ps[2][0:E, 0:128], m2tm[:, i * E:(i + 1) * E], (f"m2_{i}",), ("ps2",))
        def fn(h, i=i):
            return h.tensor_copy(m2T[:, i * 128:(i + 1) * 128], ps[2][0:E, 0:128])
        v(fn, ("ps2",), ("m2T",))

    if STOP_PHASE < 8:
        return _finish(nc, es, sch, sems, outtoks_g, ENGS)
    v(lambda h: h.tensor_tensor_scan(p1T[:, :], m1T[:, :], m1T[:, :], 0.0, ALU.add,
                                     ALU.bypass), ("m1T",), ("p1x",))
    v(lambda h: h.tensor_tensor(p1T[:, :], p1T[:, :], m1T[:, :], ALU.subtract),
      ("p1x", "m1T"), ("p1T",))
    v(lambda h: h.tensor_scalar(k1T[:, :], p1T[:, :], float(CAP), None, ALU.is_lt),
      ("p1T",), ("k1a",))
    v(lambda h: h.tensor_tensor(k1T[:, :], k1T[:, :], m1T[:, :], ALU.mult),
      ("k1a", "m1T"), ("k1T",))
    v(lambda h: h.reduce_sum(cntT[:, 0:1], k1T[:, :], axis=AX.X), ("k1T",), ("cnt1",))
    v(lambda h: h.tensor_tensor_scan(p2T[:, :], m2T[:, :], m2T[:, :], cntT[:, 0:1],
                                     ALU.add, ALU.bypass), ("m2T", "cnt1"), ("p2x",))
    v(lambda h: h.tensor_tensor(p2T[:, :], p2T[:, :], m2T[:, :], ALU.subtract),
      ("p2x", "m2T"), ("p2T",))
    v(lambda h: h.tensor_scalar(k2T[:, :], p2T[:, :], float(CAP), None, ALU.is_lt),
      ("p2T",), ("k2a",))
    v(lambda h: h.tensor_tensor(k2T[:, :], k2T[:, :], m2T[:, :], ALU.mult),
      ("k2a", "m2T"), ("k2T",))

    for i in range(16):
        specs = [(p1T, "p1T", "tp1"), (k1T, "k1T", "tk1"),
                 (p2T, "p2T", "tp2"), (k2T, "k2T", "tk2")]
        homes = {"tp1": g8a, "tk1": g8b, "tp2": g8c, "tk2": None}
        for (src, key, tag) in specs:
            trans(ps2h, src[:, i * 128:(i + 1) * 128], (key,), ("ps2",))
            home = homes[tag]
            if home is None:
                def fn(h, i=i):
                    return h.tensor_copy(m1tm[:, i * E:(i + 1) * E], ps2h)
                v(fn, ("ps2",), (tag,))
            else:
                def fn(h, home=home):
                    return h.tensor_copy(home[:], ps2h)
                v(fn, ("ps2",), (tag,))
        k2tm = m1tm[:, i * E:(i + 1) * E]
        v(lambda h: h.tensor_tensor(g8a[:], g8a[:], g8b[:], ALU.mult),
          ("tp1", "tk1"), ("pp1",))
        v(lambda h: h.reduce_sum(lncol[:, 4:5], g8a[:], axis=AX.X),
          ("pp1",), ("pos1",))
        v(lambda h: h.reduce_sum(lncol[:, 5:6], g8b[:], axis=AX.X),
          ("tk1",), ("k1f",))
        def fn(h, k2tm=k2tm):
            return h.tensor_tensor(g8c[:], g8c[:], k2tm, ALU.mult)
        v(fn, ("tp2", "tk2"), ("pp2",))
        v(lambda h: h.reduce_sum(lncol[:, 6:7], g8c[:], axis=AX.X),
          ("pp2",), ("pos2",))
        def fn(h, k2tm=k2tm):
            return h.reduce_sum(lncol[:, 7:8], k2tm, axis=AX.X)
        v(fn, ("tk2",), ("k2f",))
        def fn(h, i=i):
            return h.tensor_scalar_mul(gcols[:, i * 8 + 4:i * 8 + 5],
                                       gcols[:, i * 8 + 0:i * 8 + 1],
                                       lncol[:, 5:6])
        v(fn, ("mx1", "k1f"), (f"g1_{i}",))
        def fn(h, i=i):
            return h.tensor_scalar_mul(gcols[:, i * 8 + 5:i * 8 + 6],
                                       gcols[:, i * 8 + 1:i * 8 + 2],
                                       lncol[:, 7:8])
        v(fn, ("mx2", "k2f"), (f"g2_{i}",))
        for (c_i, c_p, c_k, oc, tag) in [(2, 4, 5, 6, f"s1_{i}"),
                                         (3, 6, 7, 7, f"s2_{i}")]:
            def fn(h, i=i, c_i=c_i, c_p=c_p):
                return h.tensor_scalar(
                    lncol[:, 8:9], gcols[:, i * 8 + c_i:i * 8 + c_i + 1], 512.0,
                    lncol[:, c_p:c_p + 1], ALU.mult, ALU.add)
            v(fn, (f"i1_{i}", f"i2_{i}", "pos1", "pos2"), ("sl0",))
            v(lambda h: h.tensor_scalar(lncol[:, 8:9], lncol[:, 8:9], -4096.0,
                                        None, ALU.add), ("sl0",), ("sl1",))
            def fn(h, c_k=c_k):
                return h.tensor_scalar_mul(lncol[:, 8:9], lncol[:, 8:9],
                                           lncol[:, c_k:c_k + 1])
            v(fn, ("sl1", "k1f", "k2f"), ("sl2",))
            def fn(h, i=i, oc=oc):
                return h.tensor_scalar(gcols[:, i * 8 + oc:i * 8 + oc + 1],
                                       lncol[:, 8:9], 4096.0, None, ALU.add)
            v(fn, ("sl2",), (tag,))
    dbg("gcols", gcols[:], tuple(f"s1_{i}" for i in range(16)))

    sreads = tuple(f"s1_{i}" for i in range(16)) + tuple(f"s2_{i}" for i in range(16))
    v(lambda h: h.tensor_copy(slotff[:, 0:16], gcols[:, 6:128:8]), sreads, ("slotff",))
    v(lambda h: h.tensor_copy(slotff[:, 16:32], gcols[:, 7:128:8]),
      sreads, ("slotff",))
    trans(ps[2][0:32, 0:128], slotff[:, 0:32], ("slotff",), ("ps2",))
    v(lambda h: h.tensor_copy(stb32[:, :], ps[2][0:32, 0:128]), ("ps2",), ("stb",))
    for q in range(2):
        def fn(h, q=q):
            return h.dma_start(
                out=idx_dram[q, :].rearrange("(i p) -> i p", i=16),
                in_=stb32[q * 16:(q + 1) * 16, :])
        DO("sync", fn, ("stb",), (f"idxd{q}",))
        def fn(h, q=q):
            return h.dma_start(
                out=slotbt[:, q * 16:(q + 1) * 16],
                in_=idx_dram[q, :].rearrange("(jj p2) -> jj p2", jj=128))
        DO("sync", fn, (f"idxd{q}",), ("slotbt",))
    trans(ps[2][0:32, 0:128], slotbt[:, 0:32], ("slotbt",), ("ps2",))
    v(lambda h: h.tensor_copy(bi16c[:, :], ps[2][0:32, 0:128]), ("ps2",), ("bi16",))
    for k in range(8):
        for q in range(2):
            def fn(h, k=k, q=q):
                return h.dma_start(
                    out=sloti[16 * k:16 * (k + 1), q * 128:(q + 1) * 128],
                    in_=bi16c[q * 16:(q + 1) * 16, :])
            DO("sync", fn, ("bi16",), ("sloti",))
    dbg("sloti", sloti, ("sloti",))

    v(lambda h: h.tensor_copy(x1bf[:, :], x1[:, :]),
      tuple(f"x1_{i}" for i in range(16)), ("x1bf",))
    for q in range(2):
        def fn(h, q=q):
            return h.dma_scatter_add(
                ei_bf, x1bf[:, :].rearrange("p (g m) -> p g m", g=16),
                sloti[:, q * 128:(q + 1) * 128], TO, TO, D, single_packet=False)
        DO("gpsimd", fn, ("x1bf", "sloti", "ei_z"), (f"ei_s{q}",))

    if STOP_PHASE < 9:
        return _finish(nc, es, sch, sems, outtoks_g, ENGS)
    eibs = [eib, eib2]
    gelbs = [gelb, gelb2]
    HW0 = 512
    chunks = []
    for sup in range(3):
        h0s = sup * 2048
        hns = min(2048, HF - h0s)
        for hc in range((hns + HW0 - 1) // HW0):
            hh0 = h0s + hc * HW0
            chunks.append((sup, hc, hh0, min(HW0, HF - hh0)))

    def eib_loads(e):
        eb = e % 2
        for mc in range(8):
            def fn(h, e=e, mc=mc, eb=eb):
                return h.dma_start(out=eibs[eb][:, mc * CAP:(mc + 1) * CAP],
                                   in_=ei_bf[e * CAP:(e + 1) * CAP,
                                              mc * 128:(mc + 1) * 128],
                                   transpose=True)
            DO("sync", fn, ("ei_s0", "ei_s1"), (f"eib{eb}",))

    eib_loads(0)
    for e in range(E):
        eb = e % 2
        ecur = eibs[eb]
        if e == 0:
            dbg("eib0", ecur, (f"eib{eb}",))
        for gc, (sup, hc, hh0, hw) in enumerate(chunks):
            dbs = gc % 2
            for wsrc, wdst, wkey, xk in ((w0, w0b, f"w0b{dbs}", ("x1_15",)),
                                         (w1, w1b, f"w1b{dbs}",
                                          ("ei_s0", "ei_s1"))):
                def fn(h, wsrc=wsrc, wdst=wdst, e=e, hh0=hh0, hw=hw, dbs=dbs):
                    return h.dma_start(
                        out=wdst[:, dbs * 4096:dbs * 4096 + 8 * hw].rearrange(
                            "p (mc hx) -> p mc hx", mc=8),
                        in_=wsrc[e, :, hh0:hh0 + hw].rearrange(
                            "(mc p) hx -> p mc hx", p=128),
                        max_dma_last_dim=4096)
                DO("gpsimd", fn, xk, (wkey,))
            for hs in range((hw + 127) // 128):
                w_ = min(128, hw - hs * 128)
                bk = hc * 4 + hs
                def fn(h, e=e, hh0=hh0, hs=hs, w_=w_, bk=bk):
                    return h.dma_start(
                        out=wob[0:w_, bk * D:(bk + 1) * D],
                        in_=wo[e, hh0 + hs * 128:hh0 + hs * 128 + w_, :],
                        max_dma_last_dim=4096)
                DO("gpsimd", fn, ("x1_15",), (f"wob{bk}",))
            for hs in range((hw + 127) // 128):
                w_ = min(128, hw - hs * 128)
                bk = hc * 4 + hs
                b0, b1 = (4, 5) if bk % 2 == 0 else (0, 1)
                gsl = bk % 2
                for mc in range(8):
                    mm(w0b[:, dbs * 4096 + mc * hw + hs * 128:
                           dbs * 4096 + mc * hw + hs * 128 + w_],
                       ecur[:, mc * CAP:(mc + 1) * CAP],
                       ps[b0][0:w_, 0:512], mc == 0, mc == 7,
                       (f"w0b{dbs}", f"eib{eb}"), (PK[b0],))
                    mm(w1b[:, dbs * 4096 + mc * hw + hs * 128:
                           dbs * 4096 + mc * hw + hs * 128 + w_],
                       ecur[:, mc * CAP:(mc + 1) * CAP],
                       ps[b1][0:w_, 0:512], mc == 0, mc == 7,
                       (f"w1b{dbs}", f"eib{eb}"), (PK[b1],))
                def fn(h, w_=w_, b0=b0, gsl=gsl):
                    return h.activation(gelbs[gsl][0:w_, :], ps[b0][0:w_, 0:512],
                                        AF.Gelu_apprx_tanh)
                s(fn, (PK[b0],), (f"gel{gsl}",))
                def fn(h, bk=bk, w_=w_, b1=b1, gsl=gsl):
                    return h.tensor_tensor(
                        hidb[0:w_, bk * CAP:(bk + 1) * CAP],
                        gelbs[gsl][0:w_, :], ps[b1][0:w_, 0:512], ALU.mult)
                v(fn, (f"gel{gsl}", PK[b1]), (f"hid{bk}",))
            if gc == 0 and e + 1 < E:
                eib_loads(e + 1)
            last_of_sup = (gc + 1 == len(chunks)) or (chunks[gc + 1][0] != sup)
            if not last_of_sup:
                continue
            nmm = []
            for (s2, hc2, hh02, hw2) in chunks:
                if s2 != sup:
                    continue
                for hs2 in range((hw2 + 127) // 128):
                    nmm.append((hc2 * 4 + hs2, min(128, hw2 - hs2 * 128)))
            for cm in range(8):
                cch, msl = cm // 2, cm % 2
                bank = cm % 2 + 6
                for q_, (bk, w_) in enumerate(nmm):
                    mm(hidb[0:w_, bk * CAP + cch * 128:bk * CAP + (cch + 1) * 128],
                       wob[0:w_, bk * D + msl * 512:bk * D + (msl + 1) * 512],
                       ps[bank][:, 0:512], q_ == 0, q_ == len(nmm) - 1,
                       (f"hid{bk}", f"wob{bk}"), (PK[bank],))
                if sup == 0:
                    def fn(h, cch=cch, msl=msl, bank=bank):
                        return h.tensor_copy(
                            eosb[:, cch * D + msl * 512:cch * D + (msl + 1) * 512],
                            ps[bank][:, 0:512])
                    v(fn, (PK[bank],), (f"eos{cm}",))
                elif sup == 1:
                    def fn(h, cch=cch, msl=msl, bank=bank):
                        return h.tensor_tensor(
                            eosb[:, cch * D + msl * 512:cch * D + (msl + 1) * 512],
                            ps[bank][:, 0:512],
                            eosb[:, cch * D + msl * 512:cch * D + (msl + 1) * 512],
                            ALU.add)
                    v(fn, (PK[bank], f"eos{cm}"), (f"eos{cm}",))
                else:
                    def fn(h, cch=cch, msl=msl, bank=bank):
                        return h.tensor_tensor(
                            eobf[:, cch * D + msl * 512:cch * D + (msl + 1) * 512],
                            ps[bank][:, 0:512],
                            eosb[:, cch * D + msl * 512:cch * D + (msl + 1) * 512],
                            ALU.add)
                    v(fn, (PK[bank], f"eos{cm}"), (f"eob{cm}",))
        if e == 0:
            dbg("eob0", eobf, tuple(f"eob{cm}" for cm in range(8)))
        def fn(h, e=e):
            return h.dma_start(
                out=eo_hbm[e * CAP:(e + 1) * CAP, :].rearrange(
                    "(g q) m -> q g m", q=128),
                in_=eobf[:, :].rearrange("q (g m) -> q g m", g=4))
        DO("sync", fn, tuple(f"eob{cm}" for cm in range(8)), ("eo_hbm",))

    if STOP_PHASE < 10:
        return _finish(nc, es, sch, sems, outtoks_g, ENGS)
    for q in range(2):
        def fn(h, q=q):
            return h.dma_gather(
                gth[:, q * 16 * D:(q + 1) * 16 * D].rearrange(
                    "p (g m) -> p g m", g=16),
                eo_hbm, sloti[:, q * 128:(q + 1) * 128], TO, TO, D,
                single_packet=False)
        DO("gpsimd", fn, ("eo_hbm", "eo_z", "sloti"), (f"gth{q}",))

    outtoks = outtoks_g
    for i in range(16):
        def fn(h, i=i):
            return h.tensor_scalar_mul(lnc2, gth[:, i * D:(i + 1) * D],
                                       gcols[:, i * 8 + 4:i * 8 + 5])
        v(fn, ("gth0", f"g1_{i}"),
          ("lnc",) + (("p1T", "p2T", "k1T", "k2T", "xt0", "xt1") if i == 0 else ()))
        def fn(h, i=i):
            return h.scalar_tensor_tensor(lnc2, gth[:, 16 * D + i * D:
                                                      16 * D + (i + 1) * D],
                                          gcols[:, i * 8 + 5:i * 8 + 6], lnc2,
                                          ALU.mult, ALU.add)
        v(fn, ("gth1", f"g2_{i}", "lnc"), ("lnc",))
        def fn(h, i=i):
            return h.tensor_tensor(lnc2, lnc2, x1[:, i * D:(i + 1) * D],
                                   ALU.add)
        v(fn, ("lnc", f"x1_{i}"), ("lnc",))
        v(lambda h: h.reduce_sum(lncol[:, 0:1], lnc2, axis=AX.X),
          ("lnc",), ("fln0",))
        v(lambda h: h.tensor_scalar_mul(lncol[:, 1:2], lncol[:, 0:1], 1.0 / D),
          ("fln0",), ("fln1",))
        def fn(h, i=i):
            return h.tensor_scalar(x1[:, i * D:(i + 1) * D], lnc2,
                                   lncol[:, 1:2], None, ALU.subtract)
        v(fn, ("lnc", "fln1"), (f"x2c{i}",))
        def fn(h, i=i):
            return h.activation(lnc2, x1[:, i * D:(i + 1) * D], AF.Square)
        s(fn, (f"x2c{i}",), ("lnc",))
        def fn(h, i=i):
            return h.reduce_sum(vcols[:, i:i + 1], lnc2, axis=AX.X)
        v(fn, ("lnc",), (f"wc{i}",))
    wreads = tuple(f"wc{i}" for i in range(16))
    v(lambda h: h.tensor_scalar(vcols[:], vcols[:], 1.0 / D, 1e-5,
                                ALU.mult, ALU.add), wreads, ("wca",))
    s(lambda h: h.activation(vcols[:], vcols[:], AF.Sqrt), ("wca",), ("wcs",))
    v(lambda h: h.reciprocal(vcols[:], vcols[:]), ("wcs",), ("wcr",))
    for i in range(16):
        def fn(h, i=i):
            return h.tensor_scalar_mul(x1[:, i * D:(i + 1) * D],
                                       x1[:, i * D:(i + 1) * D], vcols[:, i:i + 1])
        v(fn, ("wcr", f"x2c{i}"), (f"x2_{i}",))
        def fn(h, i=i):
            return h.dma_start(
                out=out_hbm.rearrange("(g q) m -> q g m", q=128)[:, i, :],
                in_=x1[:, i * D:(i + 1) * D])
        tok = DO("sync", fn, (f"x2_{i}",), (f"out{i}",))
        outtoks.append(tok)

    return _finish(nc, es, sch, sems, outtoks_g, ENGS)


def _finish(nc, es, sch, sems, outtoks, _engs):
    for tok in outtoks:
        sch.targets.add(tok)

    with es:
        block = es.enter_context(nc.Block())
        per_eng = {e: [] for e in ENGS}
        for rec in sch.ops:
            per_eng[rec[0]].append(rec)

        def make_emitter(eng, recs):
            def emit(h):
                known = {}
                cinc = 0
                for (_, fn, waits, tok, is_dma) in recs:
                    for sem_name, val in sorted(waits.items()):
                        if known.get(sem_name, 0) < val:
                            h.wait_ge(sems[sem_name], val)
                            known[sem_name] = val
                    ins = fn(h)
                    if is_dma:
                        ins.then_inc(sems[tok[0]], 16)
                    else:
                        if tok in sch.targets:
                            delta = tok[1] - cinc
                            ins.then_inc(sems["c_" + eng], delta)
                            cinc = tok[1]
                if eng in ("sync", "gpsimd"):
                    fin = {}
                    for tok in outtoks:
                        fin[tok[0]] = max(fin.get(tok[0], 0), tok[1])
                    for sem_name, val in sorted(fin.items()):
                        if known.get(sem_name, 0) < val:
                            h.wait_ge(sems[sem_name], val)
            return emit

        em = {e: make_emitter(e, per_eng[e]) for e in ENGS}

        @block.sync
        def _(h):
            em["sync"](h)

        @block.gpsimd
        def _(h):
            em["gpsimd"](h)

        @block.tensor
        def _(h):
            em["tensor"](h)

        @block.vector
        def _(h):
            em["vector"](h)

        @block.scalar
        def _(h):
            em["scalar"](h)

    lower_extended_insts(nc)
    return nc



def _host_inputs(inputs, c):
    b, hlf = c // 2, c % 2
    x = np.asarray(inputs["L0_attn_x"])
    own = x[b, hlf * TO:(hlf + 1) * TO, :]
    ctx = x[b, 0:TO, :] if hlf == 1 else np.zeros((TO, D), np.float32)
    x_fullv = np.concatenate([ctx, own], axis=0)

    pos = np.arange(4096, dtype=np.float32)
    dist = (L - 1.0) - pos
    if hlf == 1:
        posw = pos
        distw = dist
    else:
        posw = np.concatenate([np.full(TO, 1e9, np.float32), pos[0:TO]])
        distw = np.concatenate([np.full(TO, 1e9, np.float32), dist[0:TO]])
    ramp = np.empty((KH, T), np.float32)
    ramp[0:28, :] = distw[None, :]
    ramp[28:32, :] = posw[None, :]

    theta = np.asarray(inputs["L0_attn_theta"]).reshape(KH, 32, MT)
    th = np.zeros((128, KH), np.float32)
    for j in range(8):
        for m in range(MT):
            th[:, j * 4 + m] = theta[4 * j:4 * j + 4, :, m].reshape(128)

    slopes = np.concatenate([np.asarray(inputs["L0_attn_decay_slopes"]),
                             np.asarray(inputs["L0_attn_anchor_slopes"])])

    wcv = np.asarray(inputs["L0_attn_w_conv"]).reshape(4, TOTAL)
    wconv = np.zeros((128, 68), np.float32)
    bconv = np.zeros((128, 17), np.float32)
    bcv = np.asarray(inputs["L0_attn_b_conv"])
    for cc in range(17):
        wd = 128 if cc < 16 else 32
        wconv[0:wd, cc * 4:(cc + 1) * 4] = wcv[:, cc * 128:cc * 128 + wd].T
        bconv[0:wd, cc] = bcv[cc * 128:cc * 128 + wd]

    Wre = np.asarray(inputs["L0_attn_W_re"])
    Wim = np.asarray(inputs["L0_attn_W_im"])
    nsc = np.asarray(inputs["L0_attn_norm_scale"])
    perm = np.array([[h * 4 + m for h in range(32)] for m in range(4)]).reshape(128)
    Wre_m = Wre[perm]
    Wim_m = Wim[perm]
    nsr = nsc[0:128][perm].reshape(128, 1)
    nsi = nsc[128:256][perm].reshape(128, 1)

    em_ = np.zeros((KH, 8 * 128), np.float32)
    bo = np.zeros((128, 8 * KH), np.float32)
    for j in range(8):
        for col in range(128):
            em_[4 * j + col // 32, j * 128 + col] = 1.0
            bo[col, j * KH + 4 * j + col // 32] = 1.0

    cstv = np.tile(np.array([[PI / 2, TWO_PI, -TWO_PI, 0.0]], np.float32), (128, 1))

    return {
        "x_full": np.ascontiguousarray(x_fullv),
        "Win": np.asarray(inputs["L0_attn_W_in"]),
        "Wout": np.asarray(inputs["L0_attn_W_out"]),
        "theta_s": th,
        "score_col": np.asarray(inputs["L0_attn_score_scale"]).reshape(KH, 1),
        "slopes_col": slopes.reshape(KH, 1).astype(np.float32),
        "ramp32": ramp,
        "wconv_fm": wconv,
        "bconv_fm": bconv,
        "deriv": np.asarray(inputs["L0_attn_deriv_logits"]).reshape(1, 3),
        "Wre_m": np.ascontiguousarray(Wre_m),
        "Wim_m": np.ascontiguousarray(Wim_m),
        "ns_re": np.ascontiguousarray(nsr),
        "ns_im": np.ascontiguousarray(nsi),
        "router": np.asarray(inputs["L0_ffn_router_w"]),
        "w0": np.asarray(inputs["L0_ffn_w0_kernel"]),
        "w1": np.asarray(inputs["L0_ffn_w1_kernel"]),
        "wo": np.asarray(inputs["L0_ffn_wo_kernel"]),
        "ident": np.eye(128, dtype=np.float32),
        "ident16": np.eye(128, dtype=np.float16),
        "ones1": np.ones((1, 128), np.float32),
        "emat": em_.astype(ml_dtypes.bfloat16),
        "bones": bo.astype(ml_dtypes.bfloat16),
        "erow": np.tile(np.arange(E, dtype=np.float32)[None, :], (128, 1)),
        "cst": cstv,
    }


L = 4096


def kernel(**inputs):
    nc = build_program()
    in_maps = [_host_inputs(inputs, c) for c in range(8)]
    res = run_bass_kernel_spmd(nc, in_maps, core_ids=list(range(8)))
    out = np.empty((4, L, D), np.float32)
    for c in range(8):
        b, hlf = c // 2, c % 2
        out[b, hlf * TO:(hlf + 1) * TO, :] = np.asarray(res.results[c]["out"])
    return out
